# version 14
# baseline (speedup 1.0000x reference)
import numpy as np
from contextlib import ExitStack

import concourse.bass as bass
import concourse.mybir as mybir
from concourse.bass_utils import run_bass_kernel_spmd

F32 = mybir.dt.float32
F32R = mybir.dt.float32r
I32 = mybir.dt.int32
ALU = mybir.AluOpType
ACT = mybir.ActivationFunctionType

C = 1024
T = 2048
HID = 3072
DI = 2048
DS = 16
DTR = 64
CT = C // 128
EPS = 1e-5
S127 = float(np.float32(1.0 / 127.0))
LNS127 = float(np.log(np.float64(1.0) / 127.0))
ISQ = 1.0 / 32.0
NTC = 4
ARN = 33280

STOP = "x4"
CVALS = [EPS, 1.0, LNS127, -0.5, 1.0 / 1024, S127, ISQ, 127.0]



def _full_tile(tile_fn, k):
    a0 = tile_fn(k, 0)
    t = a0.tensor.ap()
    off = a0.offset
    return t[:, off:off + 2048]



class Sem:
    __slots__ = ("name", "total", "h")

    def __init__(self, name):
        self.name = name
        self.total = 0
        self.h = None

    def inc(self, n=1):
        self.total += n
        return self.total


class Prog:
    ENGINES = ("sync", "gpsimd", "tensor", "vector", "scalar")

    def __init__(self):
        self.ops = {e: [] for e in self.ENGINES}
        self.sems = {}
        self.waits = []

    def sem(self, name):
        if name not in self.sems:
            self.sems[name] = Sem(name)
        return self.sems[name]

    def wait(self, eng, sem, val):
        if val <= 0:
            return
        self.waits.append((sem, val))
        self.ops[eng].append(lambda e, s=sem, v=val: e.wait_ge(s.h, v))

    def run(self, eng, build, inc=None, incn=1):
        if inc is not None:
            inc.inc(incn)

        def fn(e, b=build, s=inc, n=incn):
            i = b(e)
            if s is not None:
                i.then_inc(s.h, n)

        self.ops[eng].append(fn)
        return inc.total if inc is not None else None

    def check(self):
        for s, v in self.waits:
            assert v <= s.total, f"wait {s.name}>={v} but total={s.total}"

    def barrier(self):
        snap = [(s, s.total) for s in self.sems.values() if s.total > 0]
        for eng in self.ENGINES:
            for s, v in snap:
                self.wait(eng, s, v)


def build_nc(stop=STOP):
    p = Prog()
    nc = bass.Bass("TRN2", target_bir_lowering=False)

    dx = nc.dram_tensor("xT", [C, T], F32R, kind="ExternalInput").ap()
    dqkv = nc.dram_tensor("wqkvT", [C, 3 * C], F32R, kind="ExternalInput").ap()
    dproj = nc.dram_tensor("wprojT", [C, C], F32R, kind="ExternalInput").ap()
    dqkvb = nc.dram_tensor("qkvb", [3 * C], F32R, kind="ExternalInput").ap()
    dprojb = nc.dram_tensor("projb", [C], F32R, kind="ExternalInput").ap()
    dnw = nc.dram_tensor("nw", [C], F32R, kind="ExternalInput").ap()
    dnb = nc.dram_tensor("nb", [C], F32R, kind="ExternalInput").ap()
    dwg = nc.dram_tensor("wgT", [C, HID], I32, kind="ExternalInput").ap()
    dwu = nc.dram_tensor("wuT", [C, HID], I32, kind="ExternalInput").ap()
    dwd = nc.dram_tensor("wdT", [HID, C], I32, kind="ExternalInput").ap()
    dwin = nc.dram_tensor("winT", [C, 2 * DI], F32R, kind="ExternalInput").ap()
    dwout = nc.dram_tensor("woutT", [DI, C], F32R, kind="ExternalInput").ap()
    dwx = nc.dram_tensor("wxT", [DI, 96], F32R, kind="ExternalInput").ap()
    dwdt = nc.dram_tensor("wdtT", [DTR, DI], F32R, kind="ExternalInput").ap()
    dalog = nc.dram_tensor("alog", [DI, DS], F32R, kind="ExternalInput").ap()
    dwc = nc.dram_tensor("wconv", [DI, 4], F32R, kind="ExternalInput").ap()
    dbc = nc.dram_tensor("bconv", [DI], F32R, kind="ExternalInput").ap()
    dbdt = nc.dram_tensor("bdt", [DI], F32R, kind="ExternalInput").ap()
    ddv = nc.dram_tensor("dvec", [DI], F32R, kind="ExternalInput").ap()
    dcv = nc.dram_tensor("cvals", [128, len(CVALS)], F32,
                         kind="ExternalInput").ap()
    don = nc.dram_tensor("onesin", [128, 128], F32R,
                         kind="ExternalInput").ap()
    dsel = nc.dram_tensor("selin", [16, 2048], F32R,
                          kind="ExternalInput").ap()
    dout = nc.dram_tensor("outT", [C, T], F32R, kind="ExternalOutput").ap()
    sc_h = nc.dram_tensor("sc_h", [C, T], F32R, kind="Internal").ap()
    sc_kv = nc.dram_tensor("sc_kv", [C, C], F32R, kind="Internal").ap()
    sc_v = nc.dram_tensor("sc_v", [T, C], F32R, kind="Internal").ap()
    sc_x = nc.dram_tensor("sc_x", [C, T], F32R, kind="Internal").ap()
    sc_res = nc.dram_tensor("sc_res", [DI, T], F32R, kind="Internal").ap()
    sc_u = nc.dram_tensor("sc_u", [DI, T], F32R, kind="Internal").ap()

    stk = ExitStack()
    try:
        xres = stk.enter_context(nc.sbuf_tensor("xres", [128, 8 * T], F32R)).ap()
        ar = stk.enter_context(nc.sbuf_tensor("ar", [128, ARN], F32R)).ap()
        consts = stk.enter_context(nc.sbuf_tensor("consts", [128, 432], F32)).ap()
        wxc = stk.enter_context(nc.sbuf_tensor("wxc", [128, 768], F32R)).ap()
        sel = stk.enter_context(nc.sbuf_tensor("sel", [16, 2048], F32R)).ap()
        ones = stk.enter_context(nc.sbuf_tensor("ones", [128, 128], F32R)).ap()
        pp = stk.enter_context(nc.psum_tensor("pp", [128, 8, 512], F32)).ap()

        acol = consts[:, 0:256]
        wcv = consts[:, 256:320]
        bcv = consts[:, 320:336]
        bdt = consts[:, 336:352]
        dvc = consts[:, 352:368]
        nwc = consts[:, 368:376]
        nbc = consts[:, 376:384]
        qbc = consts[:, 384:408]
        pbc = consts[:, 408:416]
        wxsl2 = [wxc[:, 0:384], wxc[:, 384:768]]

        state = ar[:, 18444:18508]
        da_gate = [0, 0]

        sems = {n: p.sem(n) for n in
                ("s_in", "s_w", "s_g", "s_mm", "s_sc", "s_ve", "s_st")}
        s_in, s_w, s_g, s_mm, s_sc, s_ve, s_st = (
            sems["s_in"], sems["s_w"], sems["s_g"], sems["s_mm"],
            sems["s_sc"], sems["s_ve"], sems["s_st"])

        def dma(eng, sem, dst, src, slow=False):
            p.run(eng,
                  lambda e, d=dst, s=src, sl=slow: e.dma_start(
                      out=d, in_=s, allow_slow_non_contiguous=sl),
                  inc=sem, incn=16)
            return sem.total

        def mm(bank_ap, lhsT, rhs, start, stop, inc=None):
            p.run("tensor",
                  lambda e, o=bank_ap, l=lhsT, r=rhs, st=start, sp=stop:
                  e.matmul(o, l, r, start=st, stop=sp), inc=inc)
            return inc.total if inc else None

        def sact(out, in_, func, bias=0.0, scale=1.0, inc=None):
            p.run("scalar",
                  lambda e, o=out, i=in_, f=func, b=bias, s=scale:
                  e.activation(o, i, f, bias=b, scale=s), inc=inc or s_sc)
            return (inc or s_sc).total

        for ci, cv in enumerate(CVALS):
            nc.const_aps.register(F32, cv, consts[:, 416 + ci: 417 + ci])
        dma("sync", p.sem("s_in"), consts[:, 416:416 + len(CVALS)], dcv)

        dma("sync", s_in, xres[:].rearrange("p (k t) -> p k t", k=8),
            dx.rearrange("(k p) t -> p k t", p=128))
        dma("sync", s_in, acol.rearrange("p (g n) -> p g n", g=16),
            dalog.bitcast(F32).rearrange("(g p) n -> p g n", p=128))
        dma("sync", s_in, wcv.rearrange("p (g j) -> p g j", g=16),
            dwc.bitcast(F32).rearrange("(g p) j -> p g j", p=128))
        dma("sync", s_in, bcv, dbc.bitcast(F32).rearrange("(g p) -> p g", p=128),
            slow=True)
        dma("sync", s_in, bdt, dbdt.bitcast(F32).rearrange("(g p) -> p g", p=128),
            slow=True)
        dma("sync", s_in, dvc, ddv.bitcast(F32).rearrange("(g p) -> p g", p=128),
            slow=True)
        dma("sync", s_in, nwc, dnw.bitcast(F32).rearrange("(g p) -> p g", p=128),
            slow=True)
        dma("sync", s_in, nbc, dnb.bitcast(F32).rearrange("(g p) -> p g", p=128),
            slow=True)
        dma("sync", s_in, qbc, dqkvb.bitcast(F32).rearrange("(g p) -> p g", p=128),
            slow=True)
        dma("sync", s_in, pbc, dprojb.bitcast(F32).rearrange("(g p) -> p g", p=128),
            slow=True)
        dma("sync", s_in, ones[:], don)
        dma("sync", s_in, sel[:], dsel)
        v_in = s_in.total
        c127 = consts[:, 423:424]
        s127col = consts[:, 424:425]
        p.wait("vector", s_in, s_in.total)
        p.run("vector", lambda e: e.reciprocal(s127col, c127), inc=s_ve)
        p.wait("scalar", s_in, v_in)
        sact(acol, acol, ACT.Exp)
        p.wait("vector", s_sc, s_sc.total)
        p.run("vector",
              lambda e: e.tensor_scalar(acol, acol, -1.0, None, ALU.mult),
              inc=s_ve)
        p.barrier()

        def xres_t(k, tc):
            return xres[:, k * T + tc * 512: k * T + (tc + 1) * 512]

        LW = 19968
        SQ = [LW, LW + 2048]
        MUR = LW + 4096
        VRR = LW + 6144
        MUB = LW + 8192
        RRB = LW + 10240

        def _attn_post(k):
            p.wait("vector", s_ve, s_ve.total)
            p.run("vector",
                  lambda e, k=k: e.tensor_add(
                      xres[:, k * T:(k + 1) * T], xres[:, k * T:(k + 1) * T],
                      ar[:, AH + k * T: AH + (k + 1) * T]), inc=s_ve)

        sq_last = [0]

        def emit_ln(src_tile, dst_tile, ntiles=CT, rr_bias=0.0, affine=None,
                    pre_spill=None, post_tile=None, src_marks=None,
                    scale_col=None, scratch=None, chunk_major=False,
                    bank_gate_ve=None):
            if scratch is None:
                sq0, sq1, muro, vrro, mubo, rrbo = (SQ[0], SQ[1], MUR, VRR,
                                                    MUB, RRB)
            else:
                sq0, sq1, muro, vrro, mubo, rrbo = scratch
            sqo = [sq0, sq1]
            inv = 1.0 / (128 * ntiles)
            if pre_spill is not None:
                spill_info = pre_spill()
            else:
                spill_info = None
            p.wait("tensor", s_sc, s_sc.total)
            if bank_gate_ve is not None:
                p.wait("tensor", s_ve, bank_gate_ve)
            sq_gate = [sq_last[0], sq_last[0]]
            for k in range(ntiles):
                if src_marks is not None:
                    p.wait("tensor", s_ve, src_marks[k])
                for nc_ in range(4):
                    mm(pp[0:1, 4 + nc_, :], ones[:, 0:1],
                       src_tile(k, nc_), k == 0, k == ntiles - 1,
                       inc=s_mm if (k == ntiles - 1 and nc_ == 3) else None)
            m_s1 = s_mm.total
            for k in range(ntiles):
                par = k % 2
                sq = ar[:, sqo[par]:sqo[par] + 2048]
                p.wait("scalar", s_mm, sq_gate[par])
                if src_marks is not None:
                    p.wait("scalar", s_ve, src_marks[k])
                p.run("scalar",
                      lambda e, sq=sq, st=_full_tile(src_tile, k):
                      e.activation(sq, st, ACT.Square), inc=s_sc)
                sc_sq = s_sc.total
                p.wait("tensor", s_sc, sc_sq)
                for nc_ in range(4):
                    mm(pp[0:1, nc_, :], ones[:, 0:1],
                       sq[:, 512 * nc_:512 * (nc_ + 1)],
                       k == 0, k == ntiles - 1,
                       inc=s_mm)
                sq_gate[par] = s_mm.total
            m_s2 = s_mm.total
            sq_last[0] = m_s2
            mu_r = ar[0:1, muro:muro + 2048]
            vr_r = ar[0:1, vrro:vrro + 2048]
            tmp_r = ar[0:1, sq0:sq0 + 2048]
            p.wait("scalar", s_mm, m_s1)
            for nc_ in range(4):
                sact(mu_r[:, 512 * nc_:512 * (nc_ + 1)], pp[0:1, 4 + nc_, :],
                     ACT.Copy, scale=inv)
            p.wait("scalar", s_mm, m_s2)
            for nc_ in range(4):
                sact(vr_r[:, 512 * nc_:512 * (nc_ + 1)], pp[0:1, nc_, :],
                     ACT.Copy, scale=inv)
            sc_rows = s_sc.total
            p.wait("vector", s_sc, sc_rows)
            p.run("vector", lambda e: e.tensor_mul(tmp_r, mu_r, mu_r),
                  inc=s_ve)
            p.wait("vector", s_ve, s_ve.total)
            p.run("vector", lambda e: e.tensor_sub(vr_r, vr_r, tmp_r),
                  inc=s_ve)
            p.wait("scalar", s_ve, s_ve.total)
            sact(vr_r, vr_r, ACT.Ln, bias=EPS)
            p.wait("scalar", s_sc, s_sc.total)
            sact(vr_r, vr_r, ACT.Exp, scale=-0.5, bias=rr_bias)
            sc_rr = s_sc.total
            if scale_col is not None:
                p.wait("vector", s_sc, sc_rr)
                p.run("vector",
                      lambda e: e.tensor_scalar(vr_r, vr_r, scale_col[0:1, :],
                                                None, ALU.mult), inc=s_ve)
                sc_rr_v = s_ve.total
                p.wait("tensor", s_ve, sc_rr_v)
            p.wait("tensor", s_sc, sc_rr)
            for nc_ in range(4):
                mm(pp[:, nc_, :], ones[0:1, :],
                   mu_r[:, 512 * nc_:512 * (nc_ + 1)], True, True,
                   inc=s_mm)
                mm(pp[:, 4 + nc_, :], ones[0:1, :],
                   vr_r[:, 512 * nc_:512 * (nc_ + 1)], True, True,
                   inc=s_mm)
            m_bc = s_mm.total
            p.wait("scalar", s_mm, m_bc)
            for nc_ in range(4):
                sact(ar[:, mubo + 512 * nc_:mubo + 512 * (nc_ + 1)],
                     pp[:, nc_, :], ACT.Copy)
                sact(ar[:, rrbo + 512 * nc_:rrbo + 512 * (nc_ + 1)],
                     pp[:, 4 + nc_, :], ACT.Copy)
            sc_bc = s_sc.total
            p.wait("vector", s_sc, sc_bc)
            if spill_info is not None:
                p.wait("vector", spill_info[0], spill_info[1])
            mub = ar[:, mubo:mubo + 2048]
            rrb = ar[:, rrbo:rrbo + 2048]
            if chunk_major:
                dst_marks = []
                for tcx in range(4):
                    mubc = ar[:, mubo + 512 * tcx: mubo + 512 * (tcx + 1)]
                    rrbc = ar[:, rrbo + 512 * tcx: rrbo + 512 * (tcx + 1)]
                    for k in range(ntiles):
                        dst = dst_tile(k, tcx)
                        srcc = src_tile(k, tcx)
                        p.run("vector",
                              lambda e, dst=dst, srcc=srcc, mubc=mubc:
                              e.tensor_sub(dst, srcc, mubc), inc=s_ve)
                        p.wait("vector", s_ve, s_ve.total)
                        p.run("vector",
                              lambda e, dst=dst, rrbc=rrbc:
                              e.tensor_mul(dst, dst, rrbc), inc=s_ve)
                    dst_marks.append(s_ve.total)
                p.wait("scalar", s_ve, s_ve.total)
                return dst_marks
            dst_marks = []
            for k in range(ntiles):
                dst = _full_tile(dst_tile, k)
                srcf = _full_tile(src_tile, k)
                p.run("vector",
                      lambda e, dst=dst, srcf=srcf: e.tensor_sub(
                          dst, srcf, mub), inc=s_ve)
                p.wait("vector", s_ve, s_ve.total)
                p.run("vector",
                      lambda e, dst=dst: e.tensor_mul(dst, dst, rrb),
                      inc=s_ve)
                if affine is not None:
                    wf, bf = affine
                    p.wait("vector", s_ve, s_ve.total)
                    p.run("vector",
                          lambda e, dst=dst, k=k: e.tensor_scalar(
                              dst, dst, wf(k), bf(k), ALU.mult, ALU.add),
                          inc=s_ve)
                if post_tile is not None:
                    post_tile(k)
                dst_marks.append(s_ve.total)
            p.wait("scalar", s_ve, s_ve.total)
            return dst_marks

        AH = 3584
        def y_t(k, tc):
            return ar[:, AH + k * T + tc * 512: AH + k * T + (tc + 1) * 512]

        marks1 = emit_ln(xres_t, y_t, post_tile=_attn_post)
        emit_ln(y_t, y_t, affine=(lambda k: nwc[:, k:k + 1],
                                  lambda k: nbc[:, k:k + 1]),
                src_marks=marks1)
        p.barrier()
        s_sp = p.sem("s_sp")
        g_h = dma("gpsimd", s_sp, sc_h.rearrange("(k p) t -> p k t", p=128),
                  ar[:, AH:AH + 8 * T].rearrange("p (k t) -> p k t", k=8))
        qkvrow = ar[0:1, 19968:23040]
        p.wait("sync", s_ve, s_ve.total)
        dma("sync", s_w, qkvrow, dqkvb[None, :])
        for k in range(CT):
            dma("sync", s_w, ar[:, 23040 + k * 1024: 23040 + (k + 1) * 1024],
                dqkv[128 * k:128 * (k + 1), C:2 * C])
        w_wk = s_w.total
        p.wait("tensor", s_w, w_wk)

        def h_lhs(k, i):
            return ar[:, AH + k * T + 128 * i: AH + k * T + 128 * (i + 1)]

        def k_sl(i, cg):
            return ar[:, AH + cg * T + 128 * i: AH + cg * T + 128 * (i + 1)]

        p.wait("scalar", p.sem("s_sp"), g_h)
        ev_gate = {}
        for i in range(16):
            if i >= 4:
                p.wait("tensor", s_sc, ev_gate[i - 4])
            marks = []
            for oc in range(2):
                bank = 2 * (i % 4) + oc
                for k in range(CT):
                    mm(pp[:, bank, :], h_lhs(k, i),
                       ar[:, 23040 + k * 1024 + 512 * oc:
                          23040 + k * 1024 + 512 * (oc + 1)],
                       k == 0, False)
                mm(pp[:, bank, :], ones[0:1, 0:128],
                   qkvrow[:, C + 512 * oc: C + 512 * (oc + 1)],
                   False, True, inc=s_mm)
                marks.append(s_mm.total)
            bank = 2 * (i % 4)
            p.wait("scalar", s_mm, marks[1])
            out_ap = (ar[:, AH:AH + 8 * T]
                      .rearrange("p (a t) -> p a t", a=8)
                      [:, :, 128 * i:128 * (i + 1)])
            in_ap = pp[:, bank:bank + 2, :].rearrange(
                "p b (a c) -> p (b a) c", a=4)
            p.run("scalar",
                  lambda e, o=out_ap, s=in_ap: e.activation(
                      o, s, ACT.Copy), inc=s_sc)
            ev_gate[i] = s_sc.total
        p.barrier()

        p.wait("gpsimd", p.sem("s_sp"), g_h)
        p.wait("sync", s_mm, s_mm.total)
        for k in range(CT):
            dma("sync", s_w, ar[:, 23040 + k * 1024: 23040 + (k + 1) * 1024],
                dqkv[128 * k:128 * (k + 1), 2 * C:3 * C])
        w_wv = s_w.total
        vstg2 = [ar[:, 0:1024], ar[:, 2048:3072]]
        vspill_marks = {}
        HSL = [31232, 32256, 1024]
        vst_gate = [0, 0]
        hv_gate = {}
        vev_gate = {}
        for i in range(16):
            s_h2 = p.sem(f"s_h{i % 3}")
            hs = ar[:, HSL[i % 3]:HSL[i % 3] + 1024]
            hv = hs.rearrange("p (k b) -> p k b", k=8)
            if i >= 4:
                p.wait("tensor", s_sc, vev_gate[i - 4])
            if i >= 3:
                p.wait("sync", s_mm, hv_gate[i - 3])
            dma("sync", s_h2, hv,
                sc_h.rearrange("(k p) t -> p k t", p=128)
                [:, :, 128 * i:128 * (i + 1)])
            g_hi = s_h2.total
            p.wait("tensor", s_h2, g_hi)
            if i == 0:
                p.wait("tensor", s_w, w_wv)
            marks = []
            for oc in range(2):
                bank = 2 * (i % 4) + oc
                for k in range(CT):
                    mm(pp[:, bank, :], hv[:, k, :],
                       ar[:, 23040 + k * 1024 + 512 * oc:
                          23040 + k * 1024 + 512 * (oc + 1)],
                       k == 0, False)
                mm(pp[:, bank, :], ones[0:1, 0:128],
                   qkvrow[:, 2 * C + 512 * oc: 2 * C + 512 * (oc + 1)],
                   False, True, inc=s_mm)
                marks.append(s_mm.total)
            hv_gate[i] = marks[1]
            bank = 2 * (i % 4)
            j = i % 2
            s_vo = p.sem(f"s_vo{j}")
            p.wait("scalar", s_mm, marks[1])
            if vst_gate[j]:
                p.wait("scalar", s_vo, vst_gate[j])
            p.run("scalar",
                  lambda e, j=j, bank=bank: e.activation(
                      vstg2[j], pp[:, bank:bank + 2, :], ACT.Copy),
                  inc=s_sc)
            sc_v1 = s_sc.total
            p.wait("gpsimd", s_sc, sc_v1)
            vst_gate[j] = dma(
                "gpsimd", s_vo,
                sc_v[128 * i:128 * (i + 1), :], vstg2[j])
            vev_gate[i] = s_sc.total
            vspill_marks[i] = (p.sem(f"s_vo{i % 2}"), vst_gate[i % 2])

        kvstg = [ar[:, 2560:3072], ar[:, 3072:3584]]
        vld = [ar[:, 0:512], ar[:, 512:1024], ar[:, 1024:1536],
               ar[:, 1536:2048], ar[:, 2048:2560]]
        kv_out_gate = {}
        kv_g = 0
        kv_ev_mark = s_sc.total
        for ch in range(2):
            p.wait("tensor", s_sc, kv_ev_mark)
            vld_gate = {}
            round_start = s_mm.total
            for i in range(16):
                s_vl = p.sem(f"s_vl{i % 5}")
                gate = vld_gate[i - 5] if i >= 5 else (
                    0 if ch == 0 else round_start)
                p.wait("sync", s_mm, gate)
                if ch == 0:
                    sm, sv = vspill_marks[i]
                    p.wait("sync", sm, sv)
                    if i % 5 in (0, 1):
                        sm2, sv2 = vspill_marks[14]
                        p.wait("sync", sm2, sv2)
                    if i % 5 == 4:
                        sm2, sv2 = vspill_marks[15]
                        p.wait("sync", sm2, sv2)
                dma("sync", s_vl, vld[i % 5],
                    sc_v[128 * i:128 * (i + 1), 512 * ch:512 * (ch + 1)])
                g_vi = s_vl.total
                p.wait("tensor", s_vl, g_vi)
                for cc in range(8):
                    mm(pp[:, cc, :], k_sl(i, cc), vld[i % 5],
                       i == 0, i == 15,
                       inc=s_mm if cc == 7 else None)
                vld_gate[i] = s_mm.total
            m_kv = s_mm.total
            p.wait("scalar", s_mm, m_kv)
            for cc in range(8):
                ks = kvstg[kv_g % 2]
                s_kv2 = p.sem(f"s_kv{kv_g % 2}")
                if kv_g >= 2:
                    p.wait("scalar", s_kv2, kv_out_gate[kv_g - 2])
                sact(ks, pp[:, cc, :], ACT.Copy, scale=ISQ)
                sc_e = s_sc.total
                p.wait("gpsimd", s_sc, sc_e)
                kv_out_gate[kv_g] = dma(
                    "gpsimd", s_kv2,
                    sc_kv[128 * cc:128 * (cc + 1), 512 * ch:512 * (ch + 1)],
                    ks)
                kv_g += 1
            kv_ev_mark = s_sc.total
        p.barrier()

        kvb = 3584
        s_kvld = p.sem("s_kvld")
        dma("gpsimd", s_kvld,
            ar[:, kvb:kvb + 8192].rearrange("p (k t) -> p k t", k=8),
            sc_kv.rearrange("(k p) t -> p k t", p=128))
        g_kv = s_kvld.total

        def kv_sl(k, m):
            return ar[:, kvb + k * 1024 + 128 * m: kvb + k * 1024 + 128 * (m + 1)]

        qb, ob, wb = 19968, 24064, 28160
        m_q_hist = {}
        prev_proj_marks = None
        b4_marks = []
        for tc in range(NTC):
            hb = 11776 + 4096 * (tc % 2)
            s_hc = p.sem(f"s_hc{tc % 2}")
            if tc >= 2:
                p.wait("gpsimd", s_mm, m_q_hist[tc - 2])
            dma("gpsimd", s_hc,
                ar[:, hb:hb + 4096].rearrange("p (k t) -> p k t", k=8),
                sc_h.rearrange("(k p) t -> p k t", p=128)
                [:, :, 512 * tc:512 * (tc + 1)])
            g_hc = s_hc.total

            def h_c(k, hb=hb):
                return ar[:, hb + 512 * k: hb + 512 * (k + 1)]

            kmarks = []
            wq_loaded = {}

            def load_wq(k):
                wsl_ = ar[:, wb + 1024 * (k % 2): wb + 1024 * (k % 2) + 1024]
                s_w2_ = p.sem(f"s_w{k % 2}")
                if k >= 2:
                    p.wait("sync", s_mm, kmarks[k - 2])
                elif prev_proj_marks is not None:
                    p.wait("sync", s_mm, prev_proj_marks[k + 6])
                dma("sync", s_w2_, wsl_, dqkv[128 * k:128 * (k + 1), 0:C])
                wq_loaded[k] = s_w2_.total

            load_wq(0)
            load_wq(1)
            for k in range(CT):
                wsl = ar[:, wb + 1024 * (k % 2): wb + 1024 * (k % 2) + 1024]
                s_w2 = p.sem(f"s_w{k % 2}")
                p.wait("tensor", s_w2, wq_loaded[k])
                if k == 0:
                    p.wait("tensor", s_hc, g_hc)
                    p.wait("tensor", p.sem("s_kvld"), g_kv)
                for m in range(8):
                    mm(pp[:, m, :], wsl[:, 128 * m:128 * (m + 1)], h_c(k),
                       k == 0, k == CT - 1,
                       inc=s_mm if m == 7 else None)
                kmarks.append(s_mm.total)
                if k + 2 < CT:
                    load_wq(k + 2)
            m_q = s_mm.total
            m_q_hist[tc] = m_q
            p.wait("vector", s_mm, m_q)
            q_ev = []
            for m in range(8):
                p.run("vector",
                      lambda e, m=m: e.tensor_scalar(
                          ar[:, qb + 512 * m: qb + 512 * (m + 1)],
                          pp[:, m, :], qbc[:, m:m + 1], None, ALU.add),
                      inc=s_ve)
                q_ev.append(s_ve.total)
            sc_q = s_ve.total
            for m in range(8):
                p.wait("tensor", s_ve, q_ev[m])
                for k in range(CT):
                    p.wait("tensor", s_ve, q_ev[k]) if k > m else None
                    mm(pp[:, m, :], kv_sl(k, m),
                       ar[:, qb + 512 * k: qb + 512 * (k + 1)],
                       k == 0, k == CT - 1,
                       inc=s_mm if k == CT - 1 else None)
            m_o = s_mm.total
            p.wait("scalar", s_mm, m_o)
            for m in range(8):
                sact(ar[:, ob + 512 * m: ob + 512 * (m + 1)], pp[:, m, :],
                     ACT.Copy)
            sc_o = s_sc.total
            p.wait("tensor", s_sc, sc_o)
            kmarks2 = []
            wp_loaded = {}

            def load_wp(k):
                wsl_ = ar[:, wb + 1024 * (k % 2): wb + 1024 * (k % 2) + 1024]
                s_w2_ = p.sem(f"s_w{k % 2}")
                p.wait("sync", s_mm,
                       kmarks2[k - 2] if k >= 2 else kmarks[k + 6])
                dma("sync", s_w2_, wsl_, dproj[128 * k:128 * (k + 1), :])
                wp_loaded[k] = s_w2_.total

            load_wp(0)
            load_wp(1)
            for k in range(CT):
                wsl = ar[:, wb + 1024 * (k % 2): wb + 1024 * (k % 2) + 1024]
                s_w2 = p.sem(f"s_w{k % 2}")
                p.wait("tensor", s_w2, wp_loaded[k])
                for m in range(8):
                    mm(pp[:, m, :], wsl[:, 128 * m:128 * (m + 1)],
                       ar[:, ob + 512 * k: ob + 512 * (k + 1)],
                       k == 0, k == CT - 1,
                       inc=s_mm if m == 7 else None)
                kmarks2.append(s_mm.total)
                if k + 2 < CT:
                    load_wp(k + 2)
            prev_proj_marks = kmarks2
            m_p = s_mm.total
            p.wait("vector", s_mm, m_p)
            for m in range(8):
                p.run("vector",
                      lambda e, m=m, tc=tc: e.scalar_tensor_tensor(
                          xres_t(m, tc), pp[:, m, :], pbc[:, m:m + 1],
                          xres_t(m, tc), ALU.add, ALU.add), inc=s_ve)
                if tc == NTC - 1:
                    b4_marks.append(s_ve.total)
            p.wait("tensor", s_ve, s_ve.total)

        FSCR = (3584, 5632, 7680, 9728, 11776, 13824)

        def emit_ffn(tag, src_marks=None):
            DACC, WGB, WUB, HHB, WDB, XRB = (3584, 19968, 24064, 28160,
                                             30208, 0)
            ent_mm = s_mm.total
            ent_ve = s_ve.total
            ent_sc = s_sc.total

            def spill():
                s_sp2 = p.sem("s_fx0")
                p.wait("gpsimd", s_ve, s_ve.total)
                v = dma("gpsimd", s_sp2,
                        sc_x.rearrange("(k p) t -> p k t", p=128),
                        xres.rearrange("p (k t) -> p k t", k=8))
                return (s_sp2, v)

            ln_marks = emit_ln(
                xres_t, xres_t, scale_col=s127col, pre_spill=spill,
                src_marks=src_marks, scratch=FSCR, chunk_major=True,
                bank_gate_ve=(src_marks[-1] if src_marks else None))
            ln_all = ln_marks[-1]

            pairs = [(0, 1), (2, 3), (4, 5)]
            HH2 = [HHB, 0]
            WD4 = [WDB, WDB + 1024, 32256, 2048]
            s_wd4 = p.sem(f"s_wd4{tag}")
            wd_loaded = [0]
            stage_j = [0]
            stage_evac = {}
            hh_done = {}
            hh_rd_mm = {}
            hh_wr_ve = {}
            qprev = [None]
            dacc_sc_gate = [0]
            dn_stt_prev = [0]
            first = True
            wg_gate = 0
            down_mark_c = [0]

            def wait_ev(eng, ev):
                if ev is not None:
                    p.wait(eng, s_sc if ev[0] == "sc" else s_ve, ev[1])

            def stage(kind, tc, rnd, hhb):
                j = stage_j[0]
                stage_j[0] += 1
                pr = pairs[j % 3]
                wait_ev("tensor", stage_evac.get(j - 3))
                wb = WGB if kind == "g" else WUB
                mts = (2 * rnd, 2 * rnd + 1)
                for k in range(CT):
                    for mi, mt in enumerate(mts):
                        mm(pp[:, pr[mi], :],
                           ar[:, wb + 512 * k + 128 * mt:
                              wb + 512 * k + 128 * (mt + 1)],
                           xres_t(k, tc), k == 0, k == CT - 1,
                           inc=s_mm if (k == CT - 1 and mi == 1) else None)
                m_st = s_mm.total
                if kind == "g":
                    p.wait("scalar", s_mm, m_st)
                    if (tc % 2) in hh_rd_mm:
                        p.wait("scalar", s_mm, hh_rd_mm[tc % 2])
                    if (tc % 2) in hh_wr_ve:
                        p.wait("scalar", s_ve, hh_wr_ve[tc % 2])
                    for mi, mt in enumerate(mts):
                        sact(ar[:, hhb + 512 * mt: hhb + 512 * (mt + 1)],
                             pp[:, pr[mi], :], ACT.Silu)
                    stage_evac[j] = ("sc", s_sc.total)
                else:
                    p.wait("vector", s_mm, m_st)
                    p.wait("vector", s_sc, stage_evac[j - 1][1])
                    for mi, mt in enumerate(mts):
                        p.run("vector",
                              lambda e, mt=mt, mi=mi, hhb=hhb, pr=pr:
                              e.tensor_mul(
                                  ar[:, hhb + 512 * mt: hhb + 512 * (mt + 1)],
                                  ar[:, hhb + 512 * mt: hhb + 512 * (mt + 1)],
                                  pp[:, pr[mi], :]), inc=s_ve)
                    stage_evac[j] = ("ve", s_ve.total)

            def down_qtr(tc, hhb_prev, dq):
                nonlocal first
                p.wait("tensor", s_ve, hh_done[tc])
                p.wait("tensor", s_wd4, wd_loaded[0])
                wait_ev("tensor", qprev[0])
                ms_ = (2 * dq, 2 * dq + 1)
                for kd in range(4):
                    for mi, m in enumerate(ms_):
                        mm(pp[:, 6 + mi, :],
                           ar[:, WD4[kd] + 128 * m: WD4[kd] + 128 * (m + 1)],
                           ar[:, hhb_prev + 512 * kd:
                              hhb_prev + 512 * (kd + 1)],
                           kd == 0, kd == 3,
                           inc=s_mm if (kd == 3 and mi == 1) else None)
                m_q = s_mm.total
                hh_rd_mm[tc % 2] = m_q
                down_mark_c[0] = m_q
                if first:
                    p.wait("scalar", s_mm, m_q)
                    p.wait("scalar", s_ve, ln_all)
                    for mi, m in enumerate(ms_):
                        sact(ar[:, DACC + m * T + 512 * tc:
                                DACC + m * T + 512 * (tc + 1)],
                             pp[:, 6 + mi, :], ACT.Copy, scale=s127col)
                    dacc_sc_gate[0] = s_sc.total
                    qprev[0] = ("sc", s_sc.total)
                else:
                    p.wait("vector", s_mm, m_q)
                    p.wait("vector", s_sc, dacc_sc_gate[0])
                    p.wait("vector", s_ve, dn_stt_prev[0])
                    for mi, m in enumerate(ms_):
                        p.run("vector",
                              lambda e, m=m, mi=mi, tc=tc:
                              e.scalar_tensor_tensor(
                                  ar[:, DACC + m * T + 512 * tc:
                                     DACC + m * T + 512 * (tc + 1)],
                                  pp[:, 6 + mi, :], s127col,
                                  ar[:, DACC + m * T + 512 * tc:
                                     DACC + m * T + 512 * (tc + 1)],
                                  ALU.mult, ALU.add), inc=s_ve)
                    dn_stt_prev[0] = s_ve.total
                    qprev[0] = ("ve", s_ve.total)
            for mh in range(2):
                for ms in range(3):
                    s = 3 * mh + ms
                    s_wg = p.sem(f"s_fg{s % 2}")
                    s_wu = p.sem(f"s_fu{s % 2}")
                    p.wait("gpsimd", s_mm, wg_gate)
                    if s == 0:
                        p.wait("gpsimd", s_mm, ent_mm)
                        p.wait("gpsimd", s_ve, ent_ve)
                        p.wait("gpsimd", s_sc, ent_sc)
                    dma("gpsimd", s_wg,
                        ar[:, WGB:WGB + 4096].rearrange(
                            "p (k h) -> p k h", k=8),
                        dwg.rearrange("(k p) h -> p k h", p=128)
                        [:, :, 512 * s:512 * (s + 1)])
                    w_g = s_wg.total
                    dma("gpsimd", s_wu,
                        ar[:, WUB:WUB + 4096].rearrange(
                            "p (k h) -> p k h", k=8),
                        dwu.rearrange("(k p) h -> p k h", p=128)
                        [:, :, 512 * s:512 * (s + 1)])
                    w_u = s_wu.total
                    if s > 0:
                        p.wait("gpsimd", s_mm, down_mark_c[0])
                    for kd in range(4):
                        dma("gpsimd", s_wd4, ar[:, WD4[kd]:WD4[kd] + 1024],
                            dwd[128 * (4 * s + kd):
                                128 * (4 * s + kd + 1), :])
                    wd_loaded[0] = s_wd4.total
                    p.wait("tensor", s_wg, w_g)
                    p.wait("tensor", s_wu, w_u)
                    for tc in range(NTC):
                        hhb = HH2[tc % 2]
                        p.wait("tensor", s_ve, ln_marks[tc])
                        for rnd in range(2):
                            stage("g", tc, rnd, hhb)
                            if tc > 0:
                                down_qtr(tc - 1, HH2[(tc - 1) % 2], 2 * rnd)
                            stage("u", tc, rnd, hhb)
                            if tc > 0:
                                down_qtr(tc - 1, HH2[(tc - 1) % 2],
                                         2 * rnd + 1)
                        hh_done[tc] = s_ve.total
                        hh_wr_ve[tc % 2] = s_ve.total
                        if tc == NTC - 1:
                            wg_gate = s_mm.total
                    for dq in range(4):
                        down_qtr(NTC - 1, HH2[(NTC - 1) % 2], dq)
                    first = False

            p.barrier()
            add_marks = {}
            for tc in range(NTC):
                for k in range(CT):
                    j = (tc * CT + k) % 4
                    s_xr = p.sem(f"s_fr{j}")
                    idx = tc * CT + k
                    if idx >= 4:
                        p.wait("gpsimd", s_ve, add_marks[idx - 4])
                    xstg = ar[:, XRB + 512 * j: XRB + 512 * (j + 1)]
                    v = dma("gpsimd", s_xr, xstg,
                            sc_x[128 * k:128 * (k + 1),
                                 512 * tc:512 * (tc + 1)])
                    p.wait("vector", s_xr, v)
                    p.run("vector",
                          lambda e, k=k, tc=tc, xs=xstg: e.tensor_add(
                              xres_t(k, tc), xs,
                              ar[:, DACC + k * T + 512 * tc:
                                 DACC + k * T + 512 * (tc + 1)]),
                          inc=s_ve)
                    add_marks[idx] = s_ve.total
            return [add_marks[(NTC - 1) * CT + k] for k in range(CT)]

        f1_marks = None
        if stop != "x1":
            f1_marks = emit_ffn("f1", src_marks=b4_marks)

        if stop in ("x3", "x4"):
            M_XCP = 0
            M_UQ = 8204
            M_DL = 0
            M_DU = 4096
            M_Y = 8192
            M_BCB = 12288
            M_BCC = 13312
            M_DA = [14336, 15360]
            M_RC = 16396
            M_XD = 18688
            M_RB = 20736
            M_STG = 22784
            M_WSL = 24832
            M_WDT = 28928
            M_W = 30976
            M_H = 32000

            m_ent_mm = s_mm.total
            m_ent_ve = s_ve.total
            m_ent_sc = s_sc.total

            def spill_m():
                s_sp2 = p.sem("s_fx0")
                p.wait("gpsimd", s_ve, s_ve.total)
                v = dma("gpsimd", s_sp2,
                        sc_x.rearrange("(k p) t -> p k t", p=128),
                        xres.rearrange("p (k t) -> p k t", k=8))
                return (s_sp2, v)

            ln4_marks = emit_ln(xres_t, xres_t, pre_spill=spill_m,
                                src_marks=f1_marks, scratch=FSCR,
                                chunk_major=True,
                                bank_gate_ve=f1_marks[-1])
            ln4_sc = s_sc.total

            s_win = p.sem("s_win")
            s_wxc = p.sem("s_wxc")
            s_wdtl = p.sem("s_wdtl")
            s_uo = p.sem("s_uo")

            xp_marks = {}
            pend_xproj = []

            def _emit_xproj_body(q, sc_u_done, w_wx):
                p.wait("tensor", s_sc, s_sc.total)
                p.wait("tensor", w_wx[0], w_wx[1])
                wxv = wxsl2[q % 2].rearrange("p (a b) -> p a b", a=4)
                for nc_ in range(NTC):
                    bb = 3 * (nc_ % 2)
                    groups = [(64, 0, M_XD, pp[0:64, bb, :]),
                              (16, 64, M_RB, pp[0:16, bb + 1, :]),
                              (16, 80, M_RC, pp[0:16, bb + 2, :])]
                    for rows, coff, dst, pb in groups:
                        for kq in range(4):
                            mm(pb, wxv[:, kq, coff:coff + rows],
                               ar[:, M_UQ + 2048 * kq + 512 * nc_:
                                  M_UQ + 2048 * kq + 512 * (nc_ + 1)],
                               kq == 0, kq == 3,
                               inc=s_mm if kq == 3 else None)
                        m_xp = s_mm.total
                        if q == 0:
                            p.wait("scalar", s_mm, m_xp)
                            sact(ar[0:rows, dst + 512 * nc_:
                                    dst + 512 * (nc_ + 1)], pb, ACT.Copy)
                            p.wait("tensor", s_sc, s_sc.total)
                        else:
                            p.wait("vector", s_mm, m_xp)
                            p.run("vector",
                                  lambda e, rows=rows, dst=dst, nc_=nc_,
                                  pb=pb: e.tensor_add(
                                      ar[0:rows, dst + 512 * nc_:
                                         dst + 512 * (nc_ + 1)],
                                      ar[0:rows, dst + 512 * nc_:
                                         dst + 512 * (nc_ + 1)], pb),
                                  inc=s_ve)
                            p.wait("tensor", s_ve, s_ve.total)

            m1_mark = 0
            resout_gate = [0, 0]
            ev_rd = {}
            p.wait("sync", s_mm, m_ent_mm)
            p.wait("sync", s_ve, m_ent_ve)
            p.wait("sync", s_sc, m_ent_sc)
            for q in range(4):
                p.wait("sync", s_mm, m1_mark)
                for k in range(CT):
                    dma("sync", s_win,
                        ar[:, M_WSL + 1024 * k: M_WSL + 1024 * k + 512],
                        dwin[128 * k:128 * (k + 1), 512 * q:512 * (q + 1)])
                    dma("sync", s_win,
                        ar[:, M_WSL + 1024 * k + 512: M_WSL + 1024 * (k + 1)],
                        dwin[128 * k:128 * (k + 1),
                             2048 + 512 * q:2048 + 512 * (q + 1)])
                w_win = s_win.total
                p.wait("sync", s_mm, m1_mark)
                s_wxc2 = p.sem(f"s_wxc{q % 2}")
                if q >= 2:
                    p.wait("sync", s_mm, xp_marks[q - 2])
                for kq in range(4):
                    dma("sync", s_wxc2,
                        wxsl2[q % 2].rearrange("p (a b) -> p a b", a=4)
                        [:, kq, :],
                        dwx[512 * q + 128 * kq: 512 * q + 128 * (kq + 1), :])
                w_wx = (s_wxc2, s_wxc2.total)
                p.wait("tensor", s_win, w_win)
                p.wait("vector", s_ve, s_ve.total)
                for i in range(4):
                    p.run("vector",
                          lambda e, i=i: e.tensor_scalar(
                              ar[:, M_XCP + 2051 * i: M_XCP + 2051 * i + 3],
                              ones[:, 0:3], 0.0, None, ALU.mult), inc=s_ve)
                ve_pad = s_ve.total
                for nc_ in range(NTC):
                    for rd in range(2):
                        prev = (q, nc_ - 1, rd) if nc_ > 0 else (q - 1, 3, rd)
                        p.wait("tensor", s_sc, ev_rd.get(prev, 0))
                        if q == 0:
                            p.wait("tensor", s_ve, ln4_marks[nc_])
                            if nc_ == 0 and rd == 0:
                                p.wait("tensor", s_sc, ln4_sc)
                        mts = (2 * rd, 2 * rd + 1)
                        for k in range(CT):
                            for mi, mt in enumerate(mts):
                                b0 = 4 * rd + 2 * mi
                                mm(pp[:, b0, :],
                                   ar[:, M_WSL + 1024 * k + 128 * mt:
                                      M_WSL + 1024 * k + 128 * (mt + 1)],
                                   xres_t(k, nc_), k == 0, k == CT - 1,
                                   inc=s_mm if (k == CT - 1 and mi == 1)
                                   else None)
                                mm(pp[:, b0 + 1, :],
                                   ar[:, M_WSL + 1024 * k + 512 + 128 * mt:
                                      M_WSL + 1024 * k + 512 + 128 * (mt + 1)],
                                   xres_t(k, nc_), k == 0, k == CT - 1,
                                   inc=s_mm if (k == CT - 1 and mi == 1)
                                   else None)
                        m_xr = s_mm.total
                        p.wait("scalar", s_mm, m_xr)
                        p.wait("scalar", s_ve, ve_pad)
                        for mi, mt in enumerate(mts):
                            b0 = 4 * rd + 2 * mi
                            sact(ar[:, M_XCP + 2051 * mt + 3 + 512 * nc_:
                                    M_XCP + 2051 * mt + 3 + 512 * (nc_ + 1)],
                                 pp[:, b0, :], ACT.Copy)
                            j = mt % 2
                            s_ro = p.sem(f"s_ro{j}")
                            p.wait("scalar", s_ro, resout_gate[j])
                            sact(ar[:, M_STG + 512 * j: M_STG + 512 * (j + 1)],
                                 pp[:, b0 + 1, :], ACT.Silu)
                            sc_r = s_sc.total
                            p.wait("gpsimd", s_sc, sc_r)
                            resout_gate[j] = dma(
                                "gpsimd", s_ro,
                                sc_res[512 * q + 128 * mt:
                                       512 * q + 128 * (mt + 1),
                                       512 * nc_:512 * (nc_ + 1)],
                                ar[:, M_STG + 512 * j: M_STG + 512 * (j + 1)])
                        ev_rd[(q, nc_, rd)] = s_sc.total
                if pend_xproj:
                    pend_xproj.pop(0)()
                sc_after_evac = s_sc.total
                p.wait("vector", s_sc, sc_after_evac)
                p.wait("vector", s_ve, ve_pad)
                p.wait("vector", s_uo, s_uo.total)
                if q >= 1 and (q - 1) in xp_marks:
                    p.wait("vector", s_mm, xp_marks[q - 1])
                for i in range(4):
                    g = 4 * q + i
                    ut = ar[:, M_UQ + 2048 * i: M_UQ + 2048 * (i + 1)]
                    xcp = ar[:, M_XCP + 2051 * i: M_XCP + 2051 * (i + 1)]
                    p.run("vector",
                          lambda e, u=ut, x=xcp, g=g: e.tensor_scalar(
                              u, x[:, 0:2048], wcv[:, 4 * g:4 * g + 1],
                              None, ALU.mult), inc=s_ve)
                    for j in range(1, 4):
                        p.wait("vector", s_ve, s_ve.total)
                        p.run("vector",
                              lambda e, u=ut, x=xcp, g=g, j=j:
                              e.scalar_tensor_tensor(
                                  u, x[:, j:j + 2048],
                                  wcv[:, 4 * g + j:4 * g + j + 1], u,
                                  ALU.mult, ALU.add), inc=s_ve)
                    ve_u = s_ve.total
                    p.wait("scalar", s_ve, ve_u)
                    sact(ut, ut, ACT.Silu, bias=bcv[:, g:g + 1])
                sc_u_done = s_sc.total
                p.wait("gpsimd", s_sc, sc_u_done)
                dma("gpsimd", s_uo,
                    sc_u.rearrange("(a p) t -> p a t", p=128)
                    [:, 4 * q:4 * (q + 1), :],
                    ar[:, M_UQ:M_UQ + 8192].rearrange(
                        "p (a t) -> p a t", a=4))
                def emit_xproj(q=q, sc_u_done=sc_u_done, w_wx=w_wx):
                    xp0 = s_mm.total
                    _emit_xproj_body(q, sc_u_done, w_wx)
                    xp_marks[q] = s_mm.total
                pend_xproj.append(emit_xproj)
                m1_mark = s_mm.total
            while pend_xproj:
                pend_xproj.pop(0)()
            p.barrier()
            dma("sync", s_wdtl, ar[0:64, M_WDT:M_WDT + 2048], dwdt[:, :])
            w_wdt = s_wdtl.total
            s_xrl = p.sem("s_xrl")
            g_x2 = dma("gpsimd", s_xrl,
                       xres[:].rearrange("p (k t) -> p k t", k=8),
                       sc_x.rearrange("(k p) t -> p k t", p=128))
            p.wait("tensor", s_xrl, g_x2)
            p.wait("vector", s_xrl, g_x2)
            p.barrier()

            s_url = [p.sem("s_url0"), p.sem("s_url1"), p.sem("s_url2"),
                     p.sem("s_url3")]
            url_gate = [0, 0]
            s_wo = p.sem("s_wo")
            wo_mark = 0
            m5_marks = []
            p.wait("tensor", s_wdtl, w_wdt)
            for q in range(4):
                p.wait("sync", s_mm, wo_mark)
                for kq in range(4):
                    dma("sync", s_wo,
                        ar[:, M_WSL + 1024 * kq: M_WSL + 1024 * (kq + 1)],
                        dwout[512 * q + 128 * kq:512 * q + 128 * (kq + 1), :])
                w_wo = s_wo.total
                for c in range(2):
                    for mt in range(4):
                        for nn in range(2):
                            bank = 2 * mt + nn
                            mm(pp[:, bank, :],
                               ar[0:64, M_WDT + 512 * q + 128 * mt:
                                  M_WDT + 512 * q + 128 * (mt + 1)],
                               ar[0:64, M_XD + 1024 * c + 512 * nn:
                                  M_XD + 1024 * c + 512 * (nn + 1)],
                               True, True,
                               inc=s_mm if (mt == 3 and nn == 1) else None)
                    ve_start = s_ve.total
                    du_done = [ve_start]
                    bc_done = {}
                    bc_evac_prev = [0]
                    bc_ready = {}

                    def build_bc(n):
                        par = n % 2
                        bcb_o = M_BCB if par == 0 else 22784
                        bcc_o = M_BCC if par == 0 else 23808
                        fb = ve_start if par == 0 else du_done[0]
                        p.wait("tensor", s_ve, bc_done.get(n - 2, fb))
                        p.wait("tensor", s_sc, bc_evac_prev[0])
                        for half in range(2):
                            if half == 1:
                                p.wait("tensor", s_sc, bc_evac_prev[0])
                            mm(pp[:, 6, :],
                               sel[:, 128 * n:128 * (n + 1)],
                               ar[0:16, M_RB + 1024 * c + 512 * half:
                                  M_RB + 1024 * c + 512 * (half + 1)],
                               True, True, inc=s_mm)
                            mm(pp[:, 7, :],
                               sel[:, 128 * n:128 * (n + 1)],
                               ar[0:16, M_RC + 1024 * c + 512 * half:
                                  M_RC + 1024 * c + 512 * (half + 1)],
                               True, True, inc=s_mm)
                            m_bc2 = s_mm.total
                            p.wait("scalar", s_mm, m_bc2)
                            sact(ar[:, bcb_o + 512 * half:
                                    bcb_o + 512 * (half + 1)],
                                 pp[:, 6, :], ACT.Copy)
                            sact(ar[:, bcc_o + 512 * half:
                                    bcc_o + 512 * (half + 1)],
                                 pp[:, 7, :], ACT.Copy)
                            bc_evac_prev[0] = s_sc.total
                        bc_ready[n] = s_sc.total

                    build_bc(0)
                    m_dt = s_mm.total
                    p.wait("scalar", s_mm, m_dt)
                    p.wait("scalar", s_sc, s_sc.total)
                    p.wait("scalar", s_ve, s_ve.total)
                    dl_marks = {}
                    for mt in range(4):
                        g = 4 * q + mt
                        for nn in range(2):
                            bank = 2 * mt + nn
                            dsl = ar[:, M_DL + 1024 * mt + 512 * nn:
                                     M_DL + 1024 * mt + 512 * (nn + 1)]
                            sact(dsl, pp[:, bank, :], ACT.Exp,
                                 bias=bdt[:, g:g + 1])
                            p.wait("scalar", s_sc, s_sc.total)
                            sact(dsl, dsl, ACT.Ln, bias=1.0)
                        dl_marks[mt] = s_sc.total
                    sc_dl = s_sc.total
                    p.wait("sync", s_ve, s_ve.total)
                    p.wait("sync", s_sc, s_sc.total)
                    STG4 = [M_STG, M_STG + 1024, M_DA[0], M_DA[1]]
                    uvs = []
                    for i in range(4):
                        g = 4 * q + i
                        ustg = ar[:, STG4[i]:STG4[i] + 1024]
                        v = dma("sync", s_url[i], ustg,
                                sc_u[128 * g:128 * (g + 1),
                                     1024 * c:1024 * (c + 1)])
                        uvs.append((ustg, v, i))
                    for i in range(4):
                        g = 4 * q + i
                        ustg, v, j = uvs[i]
                        p.wait("vector", s_url[j], v)
                        p.wait("vector", s_sc, dl_marks[i])
                        p.run("vector",
                              lambda e, i=i, u=ustg: e.tensor_mul(
                                  ar[:, M_DU + 1024 * i: M_DU + 1024 * (i + 1)],
                                  ar[:, M_DL + 1024 * i: M_DL + 1024 * (i + 1)],
                                  u), inc=s_ve)
                        p.run("vector",
                              lambda e, i=i, u=ustg, g=g: e.tensor_scalar(
                                  ar[:, M_Y + 1024 * i: M_Y + 1024 * (i + 1)],
                                  u, dvc[:, g:g + 1], None, ALU.mult),
                              inc=s_ve)
                    url_gate[0] = s_ve.total
                    url_gate[1] = s_ve.total
                    du_done[0] = s_ve.total
                    da_gate[0] = max(da_gate[0], s_ve.total)
                    da_gate[1] = max(da_gate[1], s_ve.total)
                    for n in range(DS):
                        par = n % 2
                        bcb_o = M_BCB if par == 0 else 22784
                        bcc_o = M_BCC if par == 0 else 23808
                        for i in range(4):
                            g = 4 * q + i
                            da = ar[:, M_DA[i % 2]:M_DA[i % 2] + 1024]
                            p.wait("scalar", s_ve, da_gate[i % 2])
                            p.run("scalar",
                                  lambda e, da=da, i=i, g=g, n=n:
                                  e.activation(
                                      da,
                                      ar[:, M_DL + 1024 * i:
                                         M_DL + 1024 * (i + 1)],
                                      ACT.Exp,
                                      scale=acol[:, 16 * g + n:
                                                 16 * g + n + 1]),
                                  inc=s_sc)
                            sc_da = s_sc.total
                            p.wait("vector", s_sc, bc_ready[n])
                            p.wait("vector", s_ve, s_ve.total)
                            p.run("vector",
                                  lambda e, i=i, bo=bcb_o: e.tensor_mul(
                                      ar[:, M_W:M_W + 1024],
                                      ar[:, M_DU + 1024 * i:
                                         M_DU + 1024 * (i + 1)],
                                      ar[:, bo:bo + 1024]), inc=s_ve)
                            p.wait("vector", s_sc, sc_da)
                            p.wait("vector", s_ve, s_ve.total)
                            init = (0.0 if c == 0 else
                                    state[:, 16 * i + n:16 * i + n + 1])
                            p.run("vector",
                                  lambda e, da=da, init=init:
                                  e.tensor_tensor_scan(
                                      ar[:, M_H:M_H + 1024], da,
                                      ar[:, M_W:M_W + 1024], init,
                                      ALU.mult, ALU.add), inc=s_ve)
                            da_gate[i % 2] = s_ve.total
                            if c == 0:
                                p.wait("vector", s_ve, s_ve.total)
                                p.run("vector",
                                      lambda e, i=i, n=n: e.tensor_copy(
                                          state[:, 16 * i + n:
                                                16 * i + n + 1],
                                          ar[:, M_H + 1023:M_H + 1024]),
                                      inc=s_ve)
                            p.wait("vector", s_ve, s_ve.total)
                            p.run("vector",
                                  lambda e, co=bcc_o: e.tensor_mul(
                                      ar[:, M_W:M_W + 1024],
                                      ar[:, M_H:M_H + 1024],
                                      ar[:, co:co + 1024]), inc=s_ve)
                            p.wait("vector", s_ve, s_ve.total)
                            p.run("vector",
                                  lambda e, i=i: e.tensor_add(
                                      ar[:, M_Y + 1024 * i:
                                         M_Y + 1024 * (i + 1)],
                                      ar[:, M_Y + 1024 * i:
                                         M_Y + 1024 * (i + 1)],
                                      ar[:, M_W:M_W + 1024]), inc=s_ve)
                            if i == 1 and n + 1 < DS:
                                build_bc(n + 1)
                        bc_done[n] = s_ve.total
                    p.wait("sync", s_ve, s_ve.total)
                    p.wait("sync", s_sc, s_sc.total)
                    STG4 = [M_STG, M_STG + 1024, M_DA[0], M_DA[1]]
                    rvs = []
                    for i in range(4):
                        g = 4 * q + i
                        ustg = ar[:, STG4[i]:STG4[i] + 1024]
                        v = dma("sync", s_url[i], ustg,
                                sc_res[128 * g:128 * (g + 1),
                                       1024 * c:1024 * (c + 1)])
                        rvs.append((ustg, v))
                    for i in range(4):
                        ustg, v = rvs[i]
                        p.wait("vector", s_url[i], v)
                        p.wait("vector", s_ve, s_ve.total)
                        p.run("vector",
                              lambda e, i=i, u=ustg: e.tensor_mul(
                                  ar[:, M_Y + 1024 * i: M_Y + 1024 * (i + 1)],
                                  ar[:, M_Y + 1024 * i: M_Y + 1024 * (i + 1)],
                                  u), inc=s_ve)
                    url_gate[0] = s_ve.total
                    url_gate[1] = s_ve.total
                    da_gate[0] = max(da_gate[0], s_ve.total)
                    da_gate[1] = max(da_gate[1], s_ve.total)
                    ve_yo = s_ve.total
                    p.wait("tensor", s_ve, ve_yo)
                    p.wait("tensor", s_wo, w_wo)
                    for nn in range(2):
                        for kq in range(4):
                            for m in range(8):
                                mm(pp[:, m, :],
                                   ar[:, M_WSL + 1024 * kq + 128 * m:
                                      M_WSL + 1024 * kq + 128 * (m + 1)],
                                   ar[:, M_Y + 1024 * kq + 512 * nn:
                                      M_Y + 1024 * kq + 512 * (nn + 1)],
                                   kq == 0, kq == 3,
                                   inc=s_mm if (kq == 3 and m == 7)
                                   else None)
                        m_wo = s_mm.total
                        p.wait("vector", s_mm, m_wo)
                        for m in range(8):
                            p.run("vector",
                                  lambda e, m=m, c=c, nn=nn:
                                  e.tensor_add(
                                      xres[:, m * T + 1024 * c + 512 * nn:
                                           m * T + 1024 * c + 512 * (nn + 1)],
                                      xres[:, m * T + 1024 * c + 512 * nn:
                                           m * T + 1024 * c + 512 * (nn + 1)],
                                      pp[:, m, :]), inc=s_ve)
                            if q == 3 and c == 1 and nn == 1:
                                m5_marks.append(s_ve.total)
                        p.wait("tensor", s_ve, s_ve.total)
                wo_mark = s_mm.total

        if stop == "x4":
            emit_ffn("f2", src_marks=m5_marks)

        p.check()

        p.wait("gpsimd", s_ve, s_ve.total)
        p.wait("gpsimd", s_sc, s_sc.total)
        dma("gpsimd", s_st, dout.rearrange("(k p) t -> p k t", p=128),
            xres[:].rearrange("p (k t) -> p k t", k=8))
        p.wait("gpsimd", s_st, s_st.total)
        p.check()

        for s in p.sems.values():
            s.h = stk.enter_context(nc.semaphore(s.name))
        block = stk.enter_context(nc.Block())

        for eng in Prog.ENGINES:
            def make(eng):
                def body(e):
                    for f in p.ops[eng]:
                        f(e)
                return body
            getattr(block, eng)(make(eng))
    finally:
        stk.close()
    return nc


def _sel_np():
    s = np.zeros((16, 2048), dtype=np.float32)
    for j in range(16):
        s[j, 128 * j:128 * (j + 1)] = 1.0
    return s


def kernel(**inputs):
    d = inputs
    x = np.ascontiguousarray(d["L0_attn_a_x"]).astype(np.float32, copy=False)
    base = {
        "wqkvT": np.ascontiguousarray(d["L0_attn_a_qkv_w"].T),
        "wprojT": np.ascontiguousarray(d["L0_attn_a_proj_w"].T),
        "qkvb": np.ascontiguousarray(d["L0_attn_a_qkv_b"]),
        "projb": np.ascontiguousarray(d["L0_attn_a_proj_b"]),
        "nw": np.ascontiguousarray(d["L0_attn_a_norm_w"]),
        "nb": np.ascontiguousarray(d["L0_attn_a_norm_b"]),
        "wgT": np.ascontiguousarray(d["L0_ffn_w_gate"].T),
        "wuT": np.ascontiguousarray(d["L0_ffn_w_up"].T),
        "wdT": np.ascontiguousarray(d["L0_ffn_w_down"].T),
        "winT": np.ascontiguousarray(d["L0_attn_b_W_in"].T),
        "woutT": np.ascontiguousarray(d["L0_attn_b_W_out"].T),
        "wxT": np.ascontiguousarray(d["L0_attn_b_W_xproj"].T),
        "wdtT": np.ascontiguousarray(d["L0_attn_b_W_dt"].T),
        "alog": np.ascontiguousarray(d["L0_attn_b_A_log"]),
        "wconv": np.ascontiguousarray(d["L0_attn_b_W_conv"][:, 0, :]),
        "bconv": np.ascontiguousarray(d["L0_attn_b_b_conv"]),
        "bdt": np.ascontiguousarray(d["L0_attn_b_b_dt"]),
        "dvec": np.ascontiguousarray(d["L0_attn_b_D"]),
        "cvals": np.tile(np.asarray(CVALS, dtype=np.float32), (128, 1)),
        "onesin": np.ones((128, 128), dtype=np.float32),
        "selin": _sel_np(),
    }
    nc = build_nc()
    in_maps = []
    for i in range(8):
        m = dict(base)
        m["xT"] = np.ascontiguousarray(x[i].T)
        in_maps.append(m)
    res = run_bass_kernel_spmd(nc, in_maps, core_ids=list(range(8)))
    out = np.stack([np.asarray(r["outT"]).T for r in res.results])
    return out.astype(np.float32)


if __name__ == "__main__":
    pass


# version 16
# speedup vs baseline: 1.0133x; 1.0133x over previous
import numpy as np
from contextlib import ExitStack

import concourse.bass as bass
import concourse.mybir as mybir
from concourse.bass_utils import run_bass_kernel_spmd

F32 = mybir.dt.float32
F32R = mybir.dt.float32r
I32 = mybir.dt.int32
ALU = mybir.AluOpType
ACT = mybir.ActivationFunctionType

C = 1024
T = 2048
HID = 3072
DI = 2048
DS = 16
DTR = 64
CT = C // 128
EPS = 1e-5
S127 = float(np.float32(1.0 / 127.0))
LNS127 = float(np.log(np.float64(1.0) / 127.0))
ISQ = 1.0 / 32.0
NTC = 4
ARN = 33280

STOP = "x4"
CVALS = [EPS, 1.0, LNS127, -0.5, 1.0 / 1024, S127, ISQ, 127.0]



def _full_tile(tile_fn, k):
    a0 = tile_fn(k, 0)
    t = a0.tensor.ap()
    off = a0.offset
    return t[:, off:off + 2048]



class Sem:
    __slots__ = ("name", "total", "h")

    def __init__(self, name):
        self.name = name
        self.total = 0
        self.h = None

    def inc(self, n=1):
        self.total += n
        return self.total


class Prog:
    ENGINES = ("sync", "gpsimd", "tensor", "vector", "scalar")

    def __init__(self):
        self.ops = {e: [] for e in self.ENGINES}
        self.sems = {}
        self.waits = []

    def sem(self, name):
        if name not in self.sems:
            self.sems[name] = Sem(name)
        return self.sems[name]

    def wait(self, eng, sem, val):
        if val <= 0:
            return
        self.waits.append((sem, val))
        self.ops[eng].append(lambda e, s=sem, v=val: e.wait_ge(s.h, v))

    def run(self, eng, build, inc=None, incn=1):
        if inc is not None:
            inc.inc(incn)

        def fn(e, b=build, s=inc, n=incn):
            i = b(e)
            if s is not None:
                i.then_inc(s.h, n)

        self.ops[eng].append(fn)
        return inc.total if inc is not None else None

    def check(self):
        for s, v in self.waits:
            assert v <= s.total, f"wait {s.name}>={v} but total={s.total}"

    def barrier(self):
        snap = [(s, s.total) for s in self.sems.values() if s.total > 0]
        for eng in self.ENGINES:
            for s, v in snap:
                self.wait(eng, s, v)


def build_nc(stop=STOP):
    p = Prog()
    nc = bass.Bass("TRN2", target_bir_lowering=False)

    dx = nc.dram_tensor("xT", [C, T], F32R, kind="ExternalInput").ap()
    dqkv = nc.dram_tensor("wqkvT", [C, 3 * C], F32R, kind="ExternalInput").ap()
    dproj = nc.dram_tensor("wprojT", [C, C], F32R, kind="ExternalInput").ap()
    dqkvb = nc.dram_tensor("qkvb", [3 * C], F32R, kind="ExternalInput").ap()
    dprojb = nc.dram_tensor("projb", [C], F32R, kind="ExternalInput").ap()
    dnw = nc.dram_tensor("nw", [C], F32R, kind="ExternalInput").ap()
    dnb = nc.dram_tensor("nb", [C], F32R, kind="ExternalInput").ap()
    dwg = nc.dram_tensor("wgT", [C, HID], I32, kind="ExternalInput").ap()
    dwu = nc.dram_tensor("wuT", [C, HID], I32, kind="ExternalInput").ap()
    dwd = nc.dram_tensor("wdT", [HID, C], I32, kind="ExternalInput").ap()
    dwin = nc.dram_tensor("winT", [C, 2 * DI], F32R, kind="ExternalInput").ap()
    dwout = nc.dram_tensor("woutT", [DI, C], F32R, kind="ExternalInput").ap()
    dwx = nc.dram_tensor("wxT", [DI, 96], F32R, kind="ExternalInput").ap()
    dwdt = nc.dram_tensor("wdtT", [DTR, DI], F32R, kind="ExternalInput").ap()
    dalog = nc.dram_tensor("alog", [DI, DS], F32R, kind="ExternalInput").ap()
    dwc = nc.dram_tensor("wconv", [DI, 4], F32R, kind="ExternalInput").ap()
    dbc = nc.dram_tensor("bconv", [DI], F32R, kind="ExternalInput").ap()
    dbdt = nc.dram_tensor("bdt", [DI], F32R, kind="ExternalInput").ap()
    ddv = nc.dram_tensor("dvec", [DI], F32R, kind="ExternalInput").ap()
    dcv = nc.dram_tensor("cvals", [128, len(CVALS)], F32,
                         kind="ExternalInput").ap()
    don = nc.dram_tensor("onesin", [128, 128], F32R,
                         kind="ExternalInput").ap()
    dsel = nc.dram_tensor("selin", [16, 2048], F32R,
                          kind="ExternalInput").ap()
    dout = nc.dram_tensor("outT", [C, T], F32R, kind="ExternalOutput").ap()
    sc_h = nc.dram_tensor("sc_h", [C, T], F32R, kind="Internal").ap()
    sc_kv = nc.dram_tensor("sc_kv", [C, C], F32R, kind="Internal").ap()
    sc_v = nc.dram_tensor("sc_v", [T, C], F32R, kind="Internal").ap()
    sc_x = nc.dram_tensor("sc_x", [C, T], F32R, kind="Internal").ap()
    sc_res = nc.dram_tensor("sc_res", [DI, T], F32R, kind="Internal").ap()
    sc_u = nc.dram_tensor("sc_u", [DI, T], F32R, kind="Internal").ap()

    stk = ExitStack()
    try:
        xres = stk.enter_context(nc.sbuf_tensor("xres", [128, 8 * T], F32R)).ap()
        ar = stk.enter_context(nc.sbuf_tensor("ar", [128, ARN], F32R)).ap()
        consts = stk.enter_context(nc.sbuf_tensor("consts", [128, 432], F32)).ap()
        wxc = stk.enter_context(nc.sbuf_tensor("wxc", [128, 768], F32R)).ap()
        sel = stk.enter_context(nc.sbuf_tensor("sel", [16, 2048], F32R)).ap()
        ones = stk.enter_context(nc.sbuf_tensor("ones", [128, 128], F32R)).ap()
        pp = stk.enter_context(nc.psum_tensor("pp", [128, 8, 512], F32)).ap()

        acol = consts[:, 0:256]
        wcv = consts[:, 256:320]
        bcv = consts[:, 320:336]
        bdt = consts[:, 336:352]
        dvc = consts[:, 352:368]
        nwc = consts[:, 368:376]
        nbc = consts[:, 376:384]
        qbc = consts[:, 384:408]
        pbc = consts[:, 408:416]
        wxsl2 = [wxc[:, 0:384], wxc[:, 384:768]]

        state = ar[:, 18444:18508]
        da_gate = [0, 0]

        sems = {n: p.sem(n) for n in
                ("s_in", "s_w", "s_g", "s_mm", "s_sc", "s_ve", "s_st")}
        s_in, s_w, s_g, s_mm, s_sc, s_ve, s_st = (
            sems["s_in"], sems["s_w"], sems["s_g"], sems["s_mm"],
            sems["s_sc"], sems["s_ve"], sems["s_st"])

        def dma(eng, sem, dst, src, slow=False):
            p.run(eng,
                  lambda e, d=dst, s=src, sl=slow: e.dma_start(
                      out=d, in_=s, allow_slow_non_contiguous=sl),
                  inc=sem, incn=16)
            return sem.total

        def mm(bank_ap, lhsT, rhs, start, stop, inc=None):
            p.run("tensor",
                  lambda e, o=bank_ap, l=lhsT, r=rhs, st=start, sp=stop:
                  e.matmul(o, l, r, start=st, stop=sp), inc=inc)
            return inc.total if inc else None

        def sact(out, in_, func, bias=0.0, scale=1.0, inc=None):
            p.run("scalar",
                  lambda e, o=out, i=in_, f=func, b=bias, s=scale:
                  e.activation(o, i, f, bias=b, scale=s), inc=inc or s_sc)
            return (inc or s_sc).total

        for ci, cv in enumerate(CVALS):
            nc.const_aps.register(F32, cv, consts[:, 416 + ci: 417 + ci])
        dma("sync", p.sem("s_in"), consts[:, 416:416 + len(CVALS)], dcv)

        dma("sync", s_in, xres[:].rearrange("p (k t) -> p k t", k=8),
            dx.rearrange("(k p) t -> p k t", p=128))
        dma("sync", s_in, acol.rearrange("p (g n) -> p g n", g=16),
            dalog.bitcast(F32).rearrange("(g p) n -> p g n", p=128))
        dma("sync", s_in, wcv.rearrange("p (g j) -> p g j", g=16),
            dwc.bitcast(F32).rearrange("(g p) j -> p g j", p=128))
        dma("sync", s_in, bcv, dbc.bitcast(F32).rearrange("(g p) -> p g", p=128),
            slow=True)
        dma("sync", s_in, bdt, dbdt.bitcast(F32).rearrange("(g p) -> p g", p=128),
            slow=True)
        dma("sync", s_in, dvc, ddv.bitcast(F32).rearrange("(g p) -> p g", p=128),
            slow=True)
        dma("sync", s_in, nwc, dnw.bitcast(F32).rearrange("(g p) -> p g", p=128),
            slow=True)
        dma("sync", s_in, nbc, dnb.bitcast(F32).rearrange("(g p) -> p g", p=128),
            slow=True)
        dma("sync", s_in, qbc, dqkvb.bitcast(F32).rearrange("(g p) -> p g", p=128),
            slow=True)
        dma("sync", s_in, pbc, dprojb.bitcast(F32).rearrange("(g p) -> p g", p=128),
            slow=True)
        dma("sync", s_in, ones[:], don)
        dma("sync", s_in, sel[:], dsel)
        v_in = s_in.total
        c127 = consts[:, 423:424]
        s127col = consts[:, 424:425]
        p.wait("vector", s_in, s_in.total)
        p.run("vector", lambda e: e.reciprocal(s127col, c127), inc=s_ve)
        p.wait("scalar", s_in, v_in)
        sact(acol, acol, ACT.Exp)
        p.wait("vector", s_sc, s_sc.total)
        p.run("vector",
              lambda e: e.tensor_scalar(acol, acol, -1.0, None, ALU.mult),
              inc=s_ve)
        p.barrier()

        def xres_t(k, tc):
            return xres[:, k * T + tc * 512: k * T + (tc + 1) * 512]

        LW = 19968
        SQ = [LW, LW + 2048]
        MUR = LW + 4096
        VRR = LW + 6144
        MUB = LW + 8192
        RRB = LW + 10240

        def _attn_post(k):
            p.wait("vector", s_ve, s_ve.total)
            p.run("vector",
                  lambda e, k=k: e.tensor_add(
                      xres[:, k * T:(k + 1) * T], xres[:, k * T:(k + 1) * T],
                      ar[:, AH + k * T: AH + (k + 1) * T]), inc=s_ve)

        sq_last = [0]
        ln_int = {}

        def emit_ln(src_tile, dst_tile, ntiles=CT, rr_bias=0.0, affine=None,
                    pre_spill=None, post_tile=None, src_marks=None,
                    scale_col=None, scratch=None, chunk_major=False,
                    bank_gate_ve=None):
            if scratch is None:
                sq0, sq1, muro, vrro, mubo, rrbo = (SQ[0], SQ[1], MUR, VRR,
                                                    MUB, RRB)
            else:
                sq0, sq1, muro, vrro, mubo, rrbo = scratch
            sqo = [sq0, sq1]
            inv = 1.0 / (128 * ntiles)
            if pre_spill is not None:
                spill_info = pre_spill()
            else:
                spill_info = None
            p.wait("tensor", s_sc, s_sc.total)
            if bank_gate_ve is not None:
                p.wait("tensor", s_ve, bank_gate_ve)
            sq_gate = [sq_last[0], sq_last[0]]
            for k in range(ntiles):
                if src_marks is not None:
                    p.wait("tensor", s_ve, src_marks[k])
                for nc_ in range(4):
                    mm(pp[0:1, 4 + nc_, :], ones[:, 0:1],
                       src_tile(k, nc_), k == 0, k == ntiles - 1,
                       inc=s_mm if (k == ntiles - 1 and nc_ == 3) else None)
            m_s1 = s_mm.total
            for k in range(ntiles):
                par = k % 2
                sq = ar[:, sqo[par]:sqo[par] + 2048]
                p.wait("scalar", s_mm, sq_gate[par])
                if src_marks is not None:
                    p.wait("scalar", s_ve, src_marks[k])
                p.run("scalar",
                      lambda e, sq=sq, st=_full_tile(src_tile, k):
                      e.activation(sq, st, ACT.Square), inc=s_sc)
                sc_sq = s_sc.total
                p.wait("tensor", s_sc, sc_sq)
                for nc_ in range(4):
                    mm(pp[0:1, nc_, :], ones[:, 0:1],
                       sq[:, 512 * nc_:512 * (nc_ + 1)],
                       k == 0, k == ntiles - 1,
                       inc=s_mm)
                sq_gate[par] = s_mm.total
            m_s2 = s_mm.total
            sq_last[0] = m_s2
            ln_int["m_s2"] = m_s2
            mu_r = ar[0:1, muro:muro + 2048]
            vr_r = ar[0:1, vrro:vrro + 2048]
            tmp_r = ar[0:1, sq0:sq0 + 2048]
            p.wait("scalar", s_mm, m_s1)
            for nc_ in range(4):
                sact(mu_r[:, 512 * nc_:512 * (nc_ + 1)], pp[0:1, 4 + nc_, :],
                     ACT.Copy, scale=inv)
            p.wait("scalar", s_mm, m_s2)
            for nc_ in range(4):
                sact(vr_r[:, 512 * nc_:512 * (nc_ + 1)], pp[0:1, nc_, :],
                     ACT.Copy, scale=inv)
            sc_rows = s_sc.total
            p.wait("vector", s_sc, sc_rows)
            p.run("vector", lambda e: e.tensor_mul(tmp_r, mu_r, mu_r),
                  inc=s_ve)
            p.wait("vector", s_ve, s_ve.total)
            p.run("vector", lambda e: e.tensor_sub(vr_r, vr_r, tmp_r),
                  inc=s_ve)
            p.wait("scalar", s_ve, s_ve.total)
            sact(vr_r, vr_r, ACT.Ln, bias=EPS)
            p.wait("scalar", s_sc, s_sc.total)
            sact(vr_r, vr_r, ACT.Exp, scale=-0.5, bias=rr_bias)
            sc_rr = s_sc.total
            ln_int["sc_rr"] = sc_rr
            if scale_col is not None:
                p.wait("vector", s_sc, sc_rr)
                p.run("vector",
                      lambda e: e.tensor_scalar(vr_r, vr_r, scale_col[0:1, :],
                                                None, ALU.mult), inc=s_ve)
                sc_rr_v = s_ve.total
                p.wait("tensor", s_ve, sc_rr_v)
            p.wait("tensor", s_sc, sc_rr)
            for nc_ in range(4):
                mm(pp[:, nc_, :], ones[0:1, :],
                   mu_r[:, 512 * nc_:512 * (nc_ + 1)], True, True,
                   inc=s_mm)
                mm(pp[:, 4 + nc_, :], ones[0:1, :],
                   vr_r[:, 512 * nc_:512 * (nc_ + 1)], True, True,
                   inc=s_mm)
            m_bc = s_mm.total
            ln_int["m_bc"] = m_bc
            p.wait("scalar", s_mm, m_bc)
            for nc_ in range(4):
                sact(ar[:, mubo + 512 * nc_:mubo + 512 * (nc_ + 1)],
                     pp[:, nc_, :], ACT.Copy)
                sact(ar[:, rrbo + 512 * nc_:rrbo + 512 * (nc_ + 1)],
                     pp[:, 4 + nc_, :], ACT.Copy)
            sc_bc = s_sc.total
            ln_int["sc_bc"] = sc_bc
            p.wait("vector", s_sc, sc_bc)
            if spill_info is not None:
                p.wait("vector", spill_info[0], spill_info[1])
            mub = ar[:, mubo:mubo + 2048]
            rrb = ar[:, rrbo:rrbo + 2048]
            if chunk_major:
                dst_marks = []
                for tcx in range(4):
                    mubc = ar[:, mubo + 512 * tcx: mubo + 512 * (tcx + 1)]
                    rrbc = ar[:, rrbo + 512 * tcx: rrbo + 512 * (tcx + 1)]
                    for k in range(ntiles):
                        dst = dst_tile(k, tcx)
                        srcc = src_tile(k, tcx)
                        p.run("vector",
                              lambda e, dst=dst, srcc=srcc, mubc=mubc:
                              e.tensor_sub(dst, srcc, mubc), inc=s_ve)
                        p.wait("vector", s_ve, s_ve.total)
                        p.run("vector",
                              lambda e, dst=dst, rrbc=rrbc:
                              e.tensor_mul(dst, dst, rrbc), inc=s_ve)
                    dst_marks.append(s_ve.total)
                p.wait("scalar", s_ve, s_ve.total)
                return dst_marks
            dst_marks = []
            for k in range(ntiles):
                dst = _full_tile(dst_tile, k)
                srcf = _full_tile(src_tile, k)
                p.run("vector",
                      lambda e, dst=dst, srcf=srcf: e.tensor_sub(
                          dst, srcf, mub), inc=s_ve)
                p.wait("vector", s_ve, s_ve.total)
                p.run("vector",
                      lambda e, dst=dst: e.tensor_mul(dst, dst, rrb),
                      inc=s_ve)
                if affine is not None:
                    wf, bf = affine
                    p.wait("vector", s_ve, s_ve.total)
                    p.run("vector",
                          lambda e, dst=dst, k=k: e.tensor_scalar(
                              dst, dst, wf(k), bf(k), ALU.mult, ALU.add),
                          inc=s_ve)
                if post_tile is not None:
                    post_tile(k)
                dst_marks.append(s_ve.total)
            p.wait("scalar", s_ve, s_ve.total)
            return dst_marks

        AH = 3584
        def y_t(k, tc):
            return ar[:, AH + k * T + tc * 512: AH + k * T + (tc + 1) * 512]

        marks1 = emit_ln(xres_t, y_t, post_tile=_attn_post)
        marks2 = emit_ln(y_t, y_t, affine=(lambda k: nwc[:, k:k + 1],
                                           lambda k: nbc[:, k:k + 1]),
                         src_marks=marks1)
        ln2i = dict(ln_int)
        s_sp = p.sem("s_sp")
        for k in range(CT):
            p.wait("gpsimd", s_ve, marks2[k])
            dma("gpsimd", s_sp, sc_h[128 * k:128 * (k + 1), :],
                ar[:, AH + k * T: AH + (k + 1) * T])
        g_h = s_sp.total
        qkvrow = ar[0:1, 19968:23040]
        p.wait("sync", s_mm, ln2i["m_s2"])
        p.wait("sync", s_sc, ln2i["sc_rr"])
        dma("sync", s_w, qkvrow, dqkvb[None, :])
        dma("sync", s_w, ar[:, 23040:24064], dqkv[0:128, C:2 * C])
        p.wait("sync", s_mm, ln2i["m_bc"])
        for k in range(1, 5):
            dma("sync", s_w, ar[:, 23040 + k * 1024: 23040 + (k + 1) * 1024],
                dqkv[128 * k:128 * (k + 1), C:2 * C])
        p.wait("sync", s_ve, s_ve.total)
        for k in range(5, CT):
            dma("sync", s_w, ar[:, 23040 + k * 1024: 23040 + (k + 1) * 1024],
                dqkv[128 * k:128 * (k + 1), C:2 * C])
        w_wk = s_w.total
        p.wait("tensor", s_w, w_wk)
        p.wait("tensor", s_sc, ln2i["sc_bc"])

        def h_lhs(k, i):
            return ar[:, AH + k * T + 128 * i: AH + k * T + 128 * (i + 1)]

        def k_sl(i, cg):
            return ar[:, AH + cg * T + 128 * i: AH + cg * T + 128 * (i + 1)]

        p.wait("scalar", p.sem("s_sp"), g_h)
        ev_gate = {}
        for i in range(16):
            if i >= 4:
                p.wait("tensor", s_sc, ev_gate[i - 4])
            marks = []
            for oc in range(2):
                bank = 2 * (i % 4) + oc
                for k in range(CT):
                    if i == 0 and oc == 0:
                        p.wait("tensor", s_ve, marks2[k])
                    mm(pp[:, bank, :], h_lhs(k, i),
                       ar[:, 23040 + k * 1024 + 512 * oc:
                          23040 + k * 1024 + 512 * (oc + 1)],
                       k == 0, False)
                mm(pp[:, bank, :], ones[0:1, 0:128],
                   qkvrow[:, C + 512 * oc: C + 512 * (oc + 1)],
                   False, True, inc=s_mm)
                marks.append(s_mm.total)
            bank = 2 * (i % 4)
            p.wait("scalar", s_mm, marks[1])
            out_ap = (ar[:, AH:AH + 8 * T]
                      .rearrange("p (a t) -> p a t", a=8)
                      [:, :, 128 * i:128 * (i + 1)])
            in_ap = pp[:, bank:bank + 2, :].rearrange(
                "p b (a c) -> p (b a) c", a=4)
            p.run("scalar",
                  lambda e, o=out_ap, s=in_ap: e.activation(
                      o, s, ACT.Copy), inc=s_sc)
            ev_gate[i] = s_sc.total
        p.barrier()

        p.wait("gpsimd", p.sem("s_sp"), g_h)
        p.wait("sync", s_mm, s_mm.total)
        for k in range(CT):
            dma("sync", s_w, ar[:, 23040 + k * 1024: 23040 + (k + 1) * 1024],
                dqkv[128 * k:128 * (k + 1), 2 * C:3 * C])
        w_wv = s_w.total
        vstg2 = [ar[:, 0:1024], ar[:, 2048:3072]]
        vspill_marks = {}
        HSL = [31232, 32256, 1024]
        vst_gate = [0, 0]
        hv_gate = {}
        vev_gate = {}
        for i in range(16):
            s_h2 = p.sem(f"s_h{i % 3}")
            hs = ar[:, HSL[i % 3]:HSL[i % 3] + 1024]
            hv = hs.rearrange("p (k b) -> p k b", k=8)
            if i >= 4:
                p.wait("tensor", s_sc, vev_gate[i - 4])
            if i >= 3:
                p.wait("sync", s_mm, hv_gate[i - 3])
            dma("sync", s_h2, hv,
                sc_h.rearrange("(k p) t -> p k t", p=128)
                [:, :, 128 * i:128 * (i + 1)])
            g_hi = s_h2.total
            p.wait("tensor", s_h2, g_hi)
            if i == 0:
                p.wait("tensor", s_w, w_wv)
            marks = []
            for oc in range(2):
                bank = 2 * (i % 4) + oc
                for k in range(CT):
                    mm(pp[:, bank, :], hv[:, k, :],
                       ar[:, 23040 + k * 1024 + 512 * oc:
                          23040 + k * 1024 + 512 * (oc + 1)],
                       k == 0, False)
                mm(pp[:, bank, :], ones[0:1, 0:128],
                   qkvrow[:, 2 * C + 512 * oc: 2 * C + 512 * (oc + 1)],
                   False, True, inc=s_mm)
                marks.append(s_mm.total)
            hv_gate[i] = marks[1]
            bank = 2 * (i % 4)
            j = i % 2
            s_vo = p.sem(f"s_vo{j}")
            p.wait("scalar", s_mm, marks[1])
            if vst_gate[j]:
                p.wait("scalar", s_vo, vst_gate[j])
            p.run("scalar",
                  lambda e, j=j, bank=bank: e.activation(
                      vstg2[j], pp[:, bank:bank + 2, :], ACT.Copy),
                  inc=s_sc)
            sc_v1 = s_sc.total
            p.wait("gpsimd", s_sc, sc_v1)
            vst_gate[j] = dma(
                "gpsimd", s_vo,
                sc_v[128 * i:128 * (i + 1), :], vstg2[j])
            vev_gate[i] = s_sc.total
            vspill_marks[i] = (p.sem(f"s_vo{i % 2}"), vst_gate[i % 2])

        kvstg = [ar[:, 2560:3072], ar[:, 3072:3584]]
        vld = [ar[:, 0:512], ar[:, 512:1024], ar[:, 1024:1536],
               ar[:, 1536:2048], ar[:, 2048:2560]]
        kv_out_gate = {}
        kv_g = 0
        kv_ev_mark = s_sc.total
        for ch in range(2):
            p.wait("tensor", s_sc, kv_ev_mark)
            vld_gate = {}
            round_start = s_mm.total
            for i in range(16):
                s_vl = p.sem(f"s_vl{i % 5}")
                gate = vld_gate[i - 5] if i >= 5 else (
                    0 if ch == 0 else round_start)
                p.wait("sync", s_mm, gate)
                if ch == 0:
                    sm, sv = vspill_marks[i]
                    p.wait("sync", sm, sv)
                    if i % 5 in (0, 1):
                        sm2, sv2 = vspill_marks[14]
                        p.wait("sync", sm2, sv2)
                    if i % 5 == 4:
                        sm2, sv2 = vspill_marks[15]
                        p.wait("sync", sm2, sv2)
                dma("sync", s_vl, vld[i % 5],
                    sc_v[128 * i:128 * (i + 1), 512 * ch:512 * (ch + 1)])
                g_vi = s_vl.total
                p.wait("tensor", s_vl, g_vi)
                for cc in range(8):
                    mm(pp[:, cc, :], k_sl(i, cc), vld[i % 5],
                       i == 0, i == 15,
                       inc=s_mm if cc == 7 else None)
                vld_gate[i] = s_mm.total
            m_kv = s_mm.total
            p.wait("scalar", s_mm, m_kv)
            for cc in range(8):
                ks = kvstg[kv_g % 2]
                s_kv2 = p.sem(f"s_kv{kv_g % 2}")
                if kv_g >= 2:
                    p.wait("scalar", s_kv2, kv_out_gate[kv_g - 2])
                sact(ks, pp[:, cc, :], ACT.Copy, scale=ISQ)
                sc_e = s_sc.total
                p.wait("gpsimd", s_sc, sc_e)
                kv_out_gate[kv_g] = dma(
                    "gpsimd", s_kv2,
                    sc_kv[128 * cc:128 * (cc + 1), 512 * ch:512 * (ch + 1)],
                    ks)
                kv_g += 1
            kv_ev_mark = s_sc.total
        p.barrier()

        kvb = 3584
        s_kvld = p.sem("s_kvld")
        dma("gpsimd", s_kvld,
            ar[:, kvb:kvb + 8192].rearrange("p (k t) -> p k t", k=8),
            sc_kv.rearrange("(k p) t -> p k t", p=128))
        g_kv = s_kvld.total

        def kv_sl(k, m):
            return ar[:, kvb + k * 1024 + 128 * m: kvb + k * 1024 + 128 * (m + 1)]

        qb, ob, wb = 19968, 24064, 28160
        m_q_hist = {}
        prev_proj_marks = None
        b4_marks = []
        for tc in range(NTC):
            hb = 11776 + 4096 * (tc % 2)
            s_hc = p.sem(f"s_hc{tc % 2}")
            if tc >= 2:
                p.wait("gpsimd", s_mm, m_q_hist[tc - 2])
            dma("gpsimd", s_hc,
                ar[:, hb:hb + 4096].rearrange("p (k t) -> p k t", k=8),
                sc_h.rearrange("(k p) t -> p k t", p=128)
                [:, :, 512 * tc:512 * (tc + 1)])
            g_hc = s_hc.total

            def h_c(k, hb=hb):
                return ar[:, hb + 512 * k: hb + 512 * (k + 1)]

            kmarks = []
            wq_loaded = {}

            def load_wq(k):
                wsl_ = ar[:, wb + 1024 * (k % 2): wb + 1024 * (k % 2) + 1024]
                s_w2_ = p.sem(f"s_w{k % 2}")
                if k >= 2:
                    p.wait("sync", s_mm, kmarks[k - 2])
                elif prev_proj_marks is not None:
                    p.wait("sync", s_mm, prev_proj_marks[k + 6])
                dma("sync", s_w2_, wsl_, dqkv[128 * k:128 * (k + 1), 0:C])
                wq_loaded[k] = s_w2_.total

            load_wq(0)
            load_wq(1)
            for k in range(CT):
                wsl = ar[:, wb + 1024 * (k % 2): wb + 1024 * (k % 2) + 1024]
                s_w2 = p.sem(f"s_w{k % 2}")
                p.wait("tensor", s_w2, wq_loaded[k])
                if k == 0:
                    p.wait("tensor", s_hc, g_hc)
                    p.wait("tensor", p.sem("s_kvld"), g_kv)
                for m in range(8):
                    mm(pp[:, m, :], wsl[:, 128 * m:128 * (m + 1)], h_c(k),
                       k == 0, k == CT - 1,
                       inc=s_mm if m == 7 else None)
                kmarks.append(s_mm.total)
                if k + 2 < CT:
                    load_wq(k + 2)
            m_q = s_mm.total
            m_q_hist[tc] = m_q
            p.wait("vector", s_mm, m_q)
            q_ev = []
            for m in range(8):
                p.run("vector",
                      lambda e, m=m: e.tensor_scalar(
                          ar[:, qb + 512 * m: qb + 512 * (m + 1)],
                          pp[:, m, :], qbc[:, m:m + 1], None, ALU.add),
                      inc=s_ve)
                q_ev.append(s_ve.total)
            sc_q = s_ve.total
            for m in range(8):
                p.wait("tensor", s_ve, q_ev[m])
                for k in range(CT):
                    p.wait("tensor", s_ve, q_ev[k]) if k > m else None
                    mm(pp[:, m, :], kv_sl(k, m),
                       ar[:, qb + 512 * k: qb + 512 * (k + 1)],
                       k == 0, k == CT - 1,
                       inc=s_mm if k == CT - 1 else None)
            m_o = s_mm.total
            p.wait("scalar", s_mm, m_o)
            for m in range(8):
                sact(ar[:, ob + 512 * m: ob + 512 * (m + 1)], pp[:, m, :],
                     ACT.Copy)
            sc_o = s_sc.total
            p.wait("tensor", s_sc, sc_o)
            kmarks2 = []
            wp_loaded = {}

            def load_wp(k):
                wsl_ = ar[:, wb + 1024 * (k % 2): wb + 1024 * (k % 2) + 1024]
                s_w2_ = p.sem(f"s_w{k % 2}")
                p.wait("sync", s_mm,
                       kmarks2[k - 2] if k >= 2 else kmarks[k + 6])
                dma("sync", s_w2_, wsl_, dproj[128 * k:128 * (k + 1), :])
                wp_loaded[k] = s_w2_.total

            load_wp(0)
            load_wp(1)
            for k in range(CT):
                wsl = ar[:, wb + 1024 * (k % 2): wb + 1024 * (k % 2) + 1024]
                s_w2 = p.sem(f"s_w{k % 2}")
                p.wait("tensor", s_w2, wp_loaded[k])
                for m in range(8):
                    mm(pp[:, m, :], wsl[:, 128 * m:128 * (m + 1)],
                       ar[:, ob + 512 * k: ob + 512 * (k + 1)],
                       k == 0, k == CT - 1,
                       inc=s_mm if m == 7 else None)
                kmarks2.append(s_mm.total)
                if k + 2 < CT:
                    load_wp(k + 2)
            prev_proj_marks = kmarks2
            m_p = s_mm.total
            p.wait("vector", s_mm, m_p)
            for m in range(8):
                p.run("vector",
                      lambda e, m=m, tc=tc: e.scalar_tensor_tensor(
                          xres_t(m, tc), pp[:, m, :], pbc[:, m:m + 1],
                          xres_t(m, tc), ALU.add, ALU.add), inc=s_ve)
                if tc == NTC - 1:
                    b4_marks.append(s_ve.total)
            p.wait("tensor", s_ve, s_ve.total)

        FSCR = (3584, 5632, 7680, 9728, 11776, 13824)

        def emit_ffn(tag, src_marks=None):
            DACC, WGB, WUB, HHB, WDB, XRB = (3584, 19968, 24064, 28160,
                                             30208, 0)
            ent_mm = s_mm.total
            ent_ve = s_ve.total
            ent_sc = s_sc.total

            def spill():
                s_sp2 = p.sem("s_fx0")
                p.wait("gpsimd", s_ve, s_ve.total)
                v = dma("gpsimd", s_sp2,
                        sc_x.rearrange("(k p) t -> p k t", p=128),
                        xres.rearrange("p (k t) -> p k t", k=8))
                return (s_sp2, v)

            ln_marks = emit_ln(
                xres_t, xres_t, scale_col=s127col, pre_spill=spill,
                src_marks=src_marks, scratch=FSCR, chunk_major=True,
                bank_gate_ve=(src_marks[-1] if src_marks else None))
            ln_all = ln_marks[-1]

            pairs = [(0, 1), (2, 3), (4, 5)]
            HH2 = [HHB, 0]
            WD4 = [WDB, WDB + 1024, 32256, 2048]
            s_wd4 = p.sem(f"s_wd4{tag}")
            wd_loaded = [0]
            stage_j = [0]
            stage_evac = {}
            hh_done = {}
            hh_rd_mm = {}
            hh_wr_ve = {}
            qprev = [None]
            dacc_sc_gate = [0]
            dn_stt_prev = [0]
            first = True
            wg_gate = 0
            down_mark_c = [0]

            def wait_ev(eng, ev):
                if ev is not None:
                    p.wait(eng, s_sc if ev[0] == "sc" else s_ve, ev[1])

            def stage(kind, tc, rnd, hhb):
                j = stage_j[0]
                stage_j[0] += 1
                pr = pairs[j % 3]
                wait_ev("tensor", stage_evac.get(j - 3))
                wb = WGB if kind == "g" else WUB
                mts = (2 * rnd, 2 * rnd + 1)
                for k in range(CT):
                    for mi, mt in enumerate(mts):
                        mm(pp[:, pr[mi], :],
                           ar[:, wb + 512 * k + 128 * mt:
                              wb + 512 * k + 128 * (mt + 1)],
                           xres_t(k, tc), k == 0, k == CT - 1,
                           inc=s_mm if (k == CT - 1 and mi == 1) else None)
                m_st = s_mm.total
                if kind == "g":
                    p.wait("scalar", s_mm, m_st)
                    if (tc % 2) in hh_rd_mm:
                        p.wait("scalar", s_mm, hh_rd_mm[tc % 2])
                    if (tc % 2) in hh_wr_ve:
                        p.wait("scalar", s_ve, hh_wr_ve[tc % 2])
                    for mi, mt in enumerate(mts):
                        sact(ar[:, hhb + 512 * mt: hhb + 512 * (mt + 1)],
                             pp[:, pr[mi], :], ACT.Silu)
                    stage_evac[j] = ("sc", s_sc.total)
                else:
                    p.wait("vector", s_mm, m_st)
                    p.wait("vector", s_sc, stage_evac[j - 1][1])
                    for mi, mt in enumerate(mts):
                        p.run("vector",
                              lambda e, mt=mt, mi=mi, hhb=hhb, pr=pr:
                              e.tensor_mul(
                                  ar[:, hhb + 512 * mt: hhb + 512 * (mt + 1)],
                                  ar[:, hhb + 512 * mt: hhb + 512 * (mt + 1)],
                                  pp[:, pr[mi], :]), inc=s_ve)
                    stage_evac[j] = ("ve", s_ve.total)

            def down_qtr(tc, hhb_prev, dq):
                nonlocal first
                p.wait("tensor", s_ve, hh_done[tc])
                p.wait("tensor", s_wd4, wd_loaded[0])
                wait_ev("tensor", qprev[0])
                ms_ = (2 * dq, 2 * dq + 1)
                for kd in range(4):
                    for mi, m in enumerate(ms_):
                        mm(pp[:, 6 + mi, :],
                           ar[:, WD4[kd] + 128 * m: WD4[kd] + 128 * (m + 1)],
                           ar[:, hhb_prev + 512 * kd:
                              hhb_prev + 512 * (kd + 1)],
                           kd == 0, kd == 3,
                           inc=s_mm if (kd == 3 and mi == 1) else None)
                m_q = s_mm.total
                hh_rd_mm[tc % 2] = m_q
                down_mark_c[0] = m_q
                if first:
                    p.wait("scalar", s_mm, m_q)
                    p.wait("scalar", s_ve, ln_all)
                    for mi, m in enumerate(ms_):
                        sact(ar[:, DACC + m * T + 512 * tc:
                                DACC + m * T + 512 * (tc + 1)],
                             pp[:, 6 + mi, :], ACT.Copy, scale=s127col)
                    dacc_sc_gate[0] = s_sc.total
                    qprev[0] = ("sc", s_sc.total)
                else:
                    p.wait("vector", s_mm, m_q)
                    p.wait("vector", s_sc, dacc_sc_gate[0])
                    p.wait("vector", s_ve, dn_stt_prev[0])
                    for mi, m in enumerate(ms_):
                        p.run("vector",
                              lambda e, m=m, mi=mi, tc=tc:
                              e.scalar_tensor_tensor(
                                  ar[:, DACC + m * T + 512 * tc:
                                     DACC + m * T + 512 * (tc + 1)],
                                  pp[:, 6 + mi, :], s127col,
                                  ar[:, DACC + m * T + 512 * tc:
                                     DACC + m * T + 512 * (tc + 1)],
                                  ALU.mult, ALU.add), inc=s_ve)
                    dn_stt_prev[0] = s_ve.total
                    qprev[0] = ("ve", s_ve.total)
            for mh in range(2):
                for ms in range(3):
                    s = 3 * mh + ms
                    s_wg = p.sem(f"s_fg{s % 2}")
                    s_wu = p.sem(f"s_fu{s % 2}")
                    p.wait("gpsimd", s_mm, wg_gate)
                    if s == 0:
                        p.wait("gpsimd", s_mm, ent_mm)
                        p.wait("gpsimd", s_ve, ent_ve)
                        p.wait("gpsimd", s_sc, ent_sc)
                    dma("gpsimd", s_wg,
                        ar[:, WGB:WGB + 4096].rearrange(
                            "p (k h) -> p k h", k=8),
                        dwg.rearrange("(k p) h -> p k h", p=128)
                        [:, :, 512 * s:512 * (s + 1)])
                    w_g = s_wg.total
                    dma("gpsimd", s_wu,
                        ar[:, WUB:WUB + 4096].rearrange(
                            "p (k h) -> p k h", k=8),
                        dwu.rearrange("(k p) h -> p k h", p=128)
                        [:, :, 512 * s:512 * (s + 1)])
                    w_u = s_wu.total
                    if s > 0:
                        p.wait("gpsimd", s_mm, down_mark_c[0])
                    for kd in range(4):
                        dma("gpsimd", s_wd4, ar[:, WD4[kd]:WD4[kd] + 1024],
                            dwd[128 * (4 * s + kd):
                                128 * (4 * s + kd + 1), :])
                    wd_loaded[0] = s_wd4.total
                    p.wait("tensor", s_wg, w_g)
                    p.wait("tensor", s_wu, w_u)
                    for tc in range(NTC):
                        hhb = HH2[tc % 2]
                        p.wait("tensor", s_ve, ln_marks[tc])
                        for rnd in range(2):
                            stage("g", tc, rnd, hhb)
                            if tc > 0:
                                down_qtr(tc - 1, HH2[(tc - 1) % 2], 2 * rnd)
                            stage("u", tc, rnd, hhb)
                            if tc > 0:
                                down_qtr(tc - 1, HH2[(tc - 1) % 2],
                                         2 * rnd + 1)
                        hh_done[tc] = s_ve.total
                        hh_wr_ve[tc % 2] = s_ve.total
                        if tc == NTC - 1:
                            wg_gate = s_mm.total
                    for dq in range(4):
                        down_qtr(NTC - 1, HH2[(NTC - 1) % 2], dq)
                    first = False

            p.barrier()
            add_marks = {}
            for tc in range(NTC):
                for k in range(CT):
                    j = (tc * CT + k) % 4
                    s_xr = p.sem(f"s_fr{j}")
                    idx = tc * CT + k
                    if idx >= 4:
                        p.wait("gpsimd", s_ve, add_marks[idx - 4])
                    xstg = ar[:, XRB + 512 * j: XRB + 512 * (j + 1)]
                    v = dma("gpsimd", s_xr, xstg,
                            sc_x[128 * k:128 * (k + 1),
                                 512 * tc:512 * (tc + 1)])
                    p.wait("vector", s_xr, v)
                    p.run("vector",
                          lambda e, k=k, tc=tc, xs=xstg: e.tensor_add(
                              xres_t(k, tc), xs,
                              ar[:, DACC + k * T + 512 * tc:
                                 DACC + k * T + 512 * (tc + 1)]),
                          inc=s_ve)
                    add_marks[idx] = s_ve.total
            return [add_marks[(NTC - 1) * CT + k] for k in range(CT)]

        f1_marks = None
        if stop != "x1":
            f1_marks = emit_ffn("f1", src_marks=b4_marks)

        if stop in ("x3", "x4"):
            M_XCP = 0
            M_UQ = 8204
            M_DL = 0
            M_DU = 4096
            M_Y = 8192
            M_BCB = 12288
            M_BCC = 13312
            M_DA = [14336, 15360]
            M_RC = 16396
            M_XD = 18688
            M_RB = 20736
            M_STG = 22784
            M_WSL = 24832
            M_WDT = 28928
            M_W = 30976
            M_H = 32000

            m_ent_mm = s_mm.total
            m_ent_ve = s_ve.total
            m_ent_sc = s_sc.total

            def spill_m():
                s_sp2 = p.sem("s_fx0")
                p.wait("gpsimd", s_ve, s_ve.total)
                v = dma("gpsimd", s_sp2,
                        sc_x.rearrange("(k p) t -> p k t", p=128),
                        xres.rearrange("p (k t) -> p k t", k=8))
                return (s_sp2, v)

            ln4_marks = emit_ln(xres_t, xres_t, pre_spill=spill_m,
                                src_marks=f1_marks, scratch=FSCR,
                                chunk_major=True,
                                bank_gate_ve=f1_marks[-1])
            ln4_sc = s_sc.total

            s_win = p.sem("s_win")
            s_wxc = p.sem("s_wxc")
            s_wdtl = p.sem("s_wdtl")
            s_uo = p.sem("s_uo")

            xp_marks = {}
            pend_xproj = []

            def _emit_xproj_body(q, sc_u_done, w_wx):
                p.wait("tensor", s_sc, s_sc.total)
                p.wait("tensor", w_wx[0], w_wx[1])
                wxv = wxsl2[q % 2].rearrange("p (a b) -> p a b", a=4)
                for nc_ in range(NTC):
                    bb = 3 * (nc_ % 2)
                    groups = [(64, 0, M_XD, pp[0:64, bb, :]),
                              (16, 64, M_RB, pp[0:16, bb + 1, :]),
                              (16, 80, M_RC, pp[0:16, bb + 2, :])]
                    for rows, coff, dst, pb in groups:
                        for kq in range(4):
                            mm(pb, wxv[:, kq, coff:coff + rows],
                               ar[:, M_UQ + 2048 * kq + 512 * nc_:
                                  M_UQ + 2048 * kq + 512 * (nc_ + 1)],
                               kq == 0, kq == 3,
                               inc=s_mm if kq == 3 else None)
                        m_xp = s_mm.total
                        if q == 0:
                            p.wait("scalar", s_mm, m_xp)
                            sact(ar[0:rows, dst + 512 * nc_:
                                    dst + 512 * (nc_ + 1)], pb, ACT.Copy)
                            p.wait("tensor", s_sc, s_sc.total)
                        else:
                            p.wait("vector", s_mm, m_xp)
                            p.run("vector",
                                  lambda e, rows=rows, dst=dst, nc_=nc_,
                                  pb=pb: e.tensor_add(
                                      ar[0:rows, dst + 512 * nc_:
                                         dst + 512 * (nc_ + 1)],
                                      ar[0:rows, dst + 512 * nc_:
                                         dst + 512 * (nc_ + 1)], pb),
                                  inc=s_ve)
                            p.wait("tensor", s_ve, s_ve.total)

            m1_mark = 0
            resout_gate = [0, 0, 0, 0]
            ev_rd = {}
            p.wait("sync", s_mm, m_ent_mm)
            p.wait("sync", s_ve, m_ent_ve)
            p.wait("sync", s_sc, m_ent_sc)
            for q in range(4):
                p.wait("sync", s_mm, m1_mark)
                for k in range(CT):
                    dma("sync", s_win,
                        ar[:, M_WSL + 1024 * k: M_WSL + 1024 * k + 512],
                        dwin[128 * k:128 * (k + 1), 512 * q:512 * (q + 1)])
                    dma("sync", s_win,
                        ar[:, M_WSL + 1024 * k + 512: M_WSL + 1024 * (k + 1)],
                        dwin[128 * k:128 * (k + 1),
                             2048 + 512 * q:2048 + 512 * (q + 1)])
                w_win = s_win.total
                p.wait("sync", s_mm, m1_mark)
                s_wxc2 = p.sem(f"s_wxc{q % 2}")
                if q >= 2:
                    p.wait("sync", s_mm, xp_marks[q - 2])
                for kq in range(4):
                    dma("sync", s_wxc2,
                        wxsl2[q % 2].rearrange("p (a b) -> p a b", a=4)
                        [:, kq, :],
                        dwx[512 * q + 128 * kq: 512 * q + 128 * (kq + 1), :])
                w_wx = (s_wxc2, s_wxc2.total)
                p.wait("tensor", s_win, w_win)
                p.wait("vector", s_ve, s_ve.total)
                for i in range(4):
                    p.run("vector",
                          lambda e, i=i: e.tensor_scalar(
                              ar[:, M_XCP + 2051 * i: M_XCP + 2051 * i + 3],
                              ones[:, 0:3], 0.0, None, ALU.mult), inc=s_ve)
                ve_pad = s_ve.total
                for nc_ in range(NTC):
                    for rd in range(2):
                        if nc_ == 1 and rd == 0 and pend_xproj:
                            pend_xproj.pop(0)()
                        prev = (q, nc_ - 1, rd) if nc_ > 0 else (q - 1, 3, rd)
                        p.wait("tensor", s_sc, ev_rd.get(prev, 0))
                        if q == 0:
                            p.wait("tensor", s_ve, ln4_marks[nc_])
                            if nc_ == 0 and rd == 0:
                                p.wait("tensor", s_sc, ln4_sc)
                        mts = (2 * rd, 2 * rd + 1)
                        for k in range(CT):
                            for mi, mt in enumerate(mts):
                                b0 = 4 * rd + 2 * mi
                                mm(pp[:, b0, :],
                                   ar[:, M_WSL + 1024 * k + 128 * mt:
                                      M_WSL + 1024 * k + 128 * (mt + 1)],
                                   xres_t(k, nc_), k == 0, k == CT - 1,
                                   inc=s_mm if (k == CT - 1 and mi == 1)
                                   else None)
                                mm(pp[:, b0 + 1, :],
                                   ar[:, M_WSL + 1024 * k + 512 + 128 * mt:
                                      M_WSL + 1024 * k + 512 + 128 * (mt + 1)],
                                   xres_t(k, nc_), k == 0, k == CT - 1,
                                   inc=s_mm if (k == CT - 1 and mi == 1)
                                   else None)
                        m_xr = s_mm.total
                        p.wait("scalar", s_mm, m_xr)
                        p.wait("scalar", s_ve, ve_pad)
                        for mi, mt in enumerate(mts):
                            b0 = 4 * rd + 2 * mi
                            sact(ar[:, M_XCP + 2051 * mt + 3 + 512 * nc_:
                                    M_XCP + 2051 * mt + 3 + 512 * (nc_ + 1)],
                                 pp[:, b0, :], ACT.Copy)
                            j = mt
                            s_ro = p.sem(f"s_ro{j}")
                            p.wait("scalar", s_ro, resout_gate[j])
                            sact(ar[:, M_STG + 512 * j: M_STG + 512 * (j + 1)],
                                 pp[:, b0 + 1, :], ACT.Silu)
                            sc_r = s_sc.total
                            p.wait("gpsimd", s_sc, sc_r)
                            resout_gate[j] = dma(
                                "gpsimd", s_ro,
                                sc_res[512 * q + 128 * mt:
                                       512 * q + 128 * (mt + 1),
                                       512 * nc_:512 * (nc_ + 1)],
                                ar[:, M_STG + 512 * j: M_STG + 512 * (j + 1)])
                        ev_rd[(q, nc_, rd)] = s_sc.total
                sc_after_evac = s_sc.total
                p.wait("vector", s_sc, sc_after_evac)
                p.wait("vector", s_ve, ve_pad)
                p.wait("vector", s_uo, s_uo.total)
                if q >= 1 and (q - 1) in xp_marks:
                    p.wait("vector", s_mm, xp_marks[q - 1])
                for i in range(4):
                    g = 4 * q + i
                    ut = ar[:, M_UQ + 2048 * i: M_UQ + 2048 * (i + 1)]
                    xcp = ar[:, M_XCP + 2051 * i: M_XCP + 2051 * (i + 1)]
                    p.run("vector",
                          lambda e, u=ut, x=xcp, g=g: e.tensor_scalar(
                              u, x[:, 0:2048], wcv[:, 4 * g:4 * g + 1],
                              None, ALU.mult), inc=s_ve)
                    for j in range(1, 4):
                        p.wait("vector", s_ve, s_ve.total)
                        p.run("vector",
                              lambda e, u=ut, x=xcp, g=g, j=j:
                              e.scalar_tensor_tensor(
                                  u, x[:, j:j + 2048],
                                  wcv[:, 4 * g + j:4 * g + j + 1], u,
                                  ALU.mult, ALU.add), inc=s_ve)
                    ve_u = s_ve.total
                    p.wait("scalar", s_ve, ve_u)
                    sact(ut, ut, ACT.Silu, bias=bcv[:, g:g + 1])
                sc_u_done = s_sc.total
                p.wait("gpsimd", s_sc, sc_u_done)
                dma("gpsimd", s_uo,
                    sc_u.rearrange("(a p) t -> p a t", p=128)
                    [:, 4 * q:4 * (q + 1), :],
                    ar[:, M_UQ:M_UQ + 8192].rearrange(
                        "p (a t) -> p a t", a=4))
                def emit_xproj(q=q, sc_u_done=sc_u_done, w_wx=w_wx):
                    xp0 = s_mm.total
                    _emit_xproj_body(q, sc_u_done, w_wx)
                    xp_marks[q] = s_mm.total
                pend_xproj.append(emit_xproj)
                m1_mark = s_mm.total
            p.wait("sync", s_mm, m1_mark)
            dma("sync", s_wdtl, ar[0:64, M_WDT:M_WDT + 2048], dwdt[:, :])
            w_wdt = s_wdtl.total
            s_xrl = p.sem("s_xrl")
            p.wait("gpsimd", s_mm, m1_mark)
            g_x2 = dma("gpsimd", s_xrl,
                       xres[:].rearrange("p (k t) -> p k t", k=8),
                       sc_x.rearrange("(k p) t -> p k t", p=128))
            while pend_xproj:
                pend_xproj.pop(0)()
            p.wait("tensor", s_xrl, g_x2)
            p.wait("vector", s_xrl, g_x2)
            p.barrier()

            s_url = [p.sem("s_url0"), p.sem("s_url1"), p.sem("s_url2"),
                     p.sem("s_url3")]
            url_gate = [0, 0]
            s_wo = p.sem("s_wo")
            wo_mark = 0
            m5_marks = []
            p.wait("tensor", s_wdtl, w_wdt)
            for q in range(4):
                p.wait("sync", s_mm, wo_mark)
                for kq in range(4):
                    dma("sync", s_wo,
                        ar[:, M_WSL + 1024 * kq: M_WSL + 1024 * (kq + 1)],
                        dwout[512 * q + 128 * kq:512 * q + 128 * (kq + 1), :])
                w_wo = s_wo.total
                for c in range(2):
                    for mt in range(4):
                        for nn in range(2):
                            bank = 2 * mt + nn
                            mm(pp[:, bank, :],
                               ar[0:64, M_WDT + 512 * q + 128 * mt:
                                  M_WDT + 512 * q + 128 * (mt + 1)],
                               ar[0:64, M_XD + 1024 * c + 512 * nn:
                                  M_XD + 1024 * c + 512 * (nn + 1)],
                               True, True,
                               inc=s_mm if (mt == 3 and nn == 1) else None)
                    ve_start = s_ve.total
                    du_done = [ve_start]
                    bc_done = {}
                    bc_evac_prev = [0]
                    bc_ready = {}

                    def build_bc(n):
                        par = n % 2
                        bcb_o = M_BCB if par == 0 else 22784
                        bcc_o = M_BCC if par == 0 else 23808
                        fb = ve_start if par == 0 else du_done[0]
                        p.wait("tensor", s_ve, bc_done.get(n - 2, fb))
                        p.wait("tensor", s_sc, bc_evac_prev[0])
                        for half in range(2):
                            if half == 1:
                                p.wait("tensor", s_sc, bc_evac_prev[0])
                            mm(pp[:, 6, :],
                               sel[:, 128 * n:128 * (n + 1)],
                               ar[0:16, M_RB + 1024 * c + 512 * half:
                                  M_RB + 1024 * c + 512 * (half + 1)],
                               True, True, inc=s_mm)
                            mm(pp[:, 7, :],
                               sel[:, 128 * n:128 * (n + 1)],
                               ar[0:16, M_RC + 1024 * c + 512 * half:
                                  M_RC + 1024 * c + 512 * (half + 1)],
                               True, True, inc=s_mm)
                            m_bc2 = s_mm.total
                            p.wait("scalar", s_mm, m_bc2)
                            sact(ar[:, bcb_o + 512 * half:
                                    bcb_o + 512 * (half + 1)],
                                 pp[:, 6, :], ACT.Copy)
                            sact(ar[:, bcc_o + 512 * half:
                                    bcc_o + 512 * (half + 1)],
                                 pp[:, 7, :], ACT.Copy)
                            bc_evac_prev[0] = s_sc.total
                        bc_ready[n] = s_sc.total

                    build_bc(0)
                    m_dt = s_mm.total
                    p.wait("scalar", s_mm, m_dt)
                    p.wait("scalar", s_sc, s_sc.total)
                    p.wait("scalar", s_ve, s_ve.total)
                    dl_marks = {}
                    for mt in range(4):
                        g = 4 * q + mt
                        for nn in range(2):
                            bank = 2 * mt + nn
                            dsl = ar[:, M_DL + 1024 * mt + 512 * nn:
                                     M_DL + 1024 * mt + 512 * (nn + 1)]
                            sact(dsl, pp[:, bank, :], ACT.Exp,
                                 bias=bdt[:, g:g + 1])
                            p.wait("scalar", s_sc, s_sc.total)
                            sact(dsl, dsl, ACT.Ln, bias=1.0)
                        dl_marks[mt] = s_sc.total
                    sc_dl = s_sc.total
                    p.wait("sync", s_ve, s_ve.total)
                    p.wait("sync", s_sc, s_sc.total)
                    STG4 = [M_STG, M_STG + 1024, M_DA[0], M_DA[1]]
                    uvs = []
                    for i in range(4):
                        g = 4 * q + i
                        ustg = ar[:, STG4[i]:STG4[i] + 1024]
                        v = dma("sync", s_url[i], ustg,
                                sc_u[128 * g:128 * (g + 1),
                                     1024 * c:1024 * (c + 1)])
                        uvs.append((ustg, v, i))
                    for i in range(4):
                        g = 4 * q + i
                        ustg, v, j = uvs[i]
                        p.wait("vector", s_url[j], v)
                        p.wait("vector", s_sc, dl_marks[i])
                        p.run("vector",
                              lambda e, i=i, u=ustg: e.tensor_mul(
                                  ar[:, M_DU + 1024 * i: M_DU + 1024 * (i + 1)],
                                  ar[:, M_DL + 1024 * i: M_DL + 1024 * (i + 1)],
                                  u), inc=s_ve)
                        p.run("vector",
                              lambda e, i=i, u=ustg, g=g: e.tensor_scalar(
                                  ar[:, M_Y + 1024 * i: M_Y + 1024 * (i + 1)],
                                  u, dvc[:, g:g + 1], None, ALU.mult),
                              inc=s_ve)
                    url_gate[0] = s_ve.total
                    url_gate[1] = s_ve.total
                    du_done[0] = s_ve.total
                    da_gate[0] = max(da_gate[0], s_ve.total)
                    da_gate[1] = max(da_gate[1], s_ve.total)
                    for n in range(DS):
                        par = n % 2
                        bcb_o = M_BCB if par == 0 else 22784
                        bcc_o = M_BCC if par == 0 else 23808
                        for i in range(4):
                            g = 4 * q + i
                            da = ar[:, M_DA[i % 2]:M_DA[i % 2] + 1024]
                            p.wait("scalar", s_ve, da_gate[i % 2])
                            p.run("scalar",
                                  lambda e, da=da, i=i, g=g, n=n:
                                  e.activation(
                                      da,
                                      ar[:, M_DL + 1024 * i:
                                         M_DL + 1024 * (i + 1)],
                                      ACT.Exp,
                                      scale=acol[:, 16 * g + n:
                                                 16 * g + n + 1]),
                                  inc=s_sc)
                            sc_da = s_sc.total
                            p.wait("vector", s_sc, bc_ready[n])
                            p.wait("vector", s_ve, s_ve.total)
                            p.run("vector",
                                  lambda e, i=i, bo=bcb_o: e.tensor_mul(
                                      ar[:, M_W:M_W + 1024],
                                      ar[:, M_DU + 1024 * i:
                                         M_DU + 1024 * (i + 1)],
                                      ar[:, bo:bo + 1024]), inc=s_ve)
                            p.wait("vector", s_sc, sc_da)
                            p.wait("vector", s_ve, s_ve.total)
                            init = (0.0 if c == 0 else
                                    state[:, 16 * i + n:16 * i + n + 1])
                            p.run("vector",
                                  lambda e, da=da, init=init:
                                  e.tensor_tensor_scan(
                                      ar[:, M_H:M_H + 1024], da,
                                      ar[:, M_W:M_W + 1024], init,
                                      ALU.mult, ALU.add), inc=s_ve)
                            da_gate[i % 2] = s_ve.total
                            if c == 0:
                                p.wait("vector", s_ve, s_ve.total)
                                p.run("vector",
                                      lambda e, i=i, n=n: e.tensor_copy(
                                          state[:, 16 * i + n:
                                                16 * i + n + 1],
                                          ar[:, M_H + 1023:M_H + 1024]),
                                      inc=s_ve)
                            p.wait("vector", s_ve, s_ve.total)
                            p.run("vector",
                                  lambda e, co=bcc_o: e.tensor_mul(
                                      ar[:, M_W:M_W + 1024],
                                      ar[:, M_H:M_H + 1024],
                                      ar[:, co:co + 1024]), inc=s_ve)
                            p.wait("vector", s_ve, s_ve.total)
                            p.run("vector",
                                  lambda e, i=i: e.tensor_add(
                                      ar[:, M_Y + 1024 * i:
                                         M_Y + 1024 * (i + 1)],
                                      ar[:, M_Y + 1024 * i:
                                         M_Y + 1024 * (i + 1)],
                                      ar[:, M_W:M_W + 1024]), inc=s_ve)
                            if i == 1 and n + 1 < DS:
                                build_bc(n + 1)
                        bc_done[n] = s_ve.total
                    p.wait("sync", s_ve, s_ve.total)
                    p.wait("sync", s_sc, s_sc.total)
                    STG4 = [M_STG, M_STG + 1024, M_DA[0], M_DA[1]]
                    rvs = []
                    for i in range(4):
                        g = 4 * q + i
                        ustg = ar[:, STG4[i]:STG4[i] + 1024]
                        v = dma("sync", s_url[i], ustg,
                                sc_res[128 * g:128 * (g + 1),
                                       1024 * c:1024 * (c + 1)])
                        rvs.append((ustg, v))
                    for i in range(4):
                        ustg, v = rvs[i]
                        p.wait("vector", s_url[i], v)
                        p.wait("vector", s_ve, s_ve.total)
                        p.run("vector",
                              lambda e, i=i, u=ustg: e.tensor_mul(
                                  ar[:, M_Y + 1024 * i: M_Y + 1024 * (i + 1)],
                                  ar[:, M_Y + 1024 * i: M_Y + 1024 * (i + 1)],
                                  u), inc=s_ve)
                    url_gate[0] = s_ve.total
                    url_gate[1] = s_ve.total
                    da_gate[0] = max(da_gate[0], s_ve.total)
                    da_gate[1] = max(da_gate[1], s_ve.total)
                    ve_yo = s_ve.total
                    p.wait("tensor", s_ve, ve_yo)
                    p.wait("tensor", s_wo, w_wo)
                    for nn in range(2):
                        for kq in range(4):
                            for m in range(8):
                                mm(pp[:, m, :],
                                   ar[:, M_WSL + 1024 * kq + 128 * m:
                                      M_WSL + 1024 * kq + 128 * (m + 1)],
                                   ar[:, M_Y + 1024 * kq + 512 * nn:
                                      M_Y + 1024 * kq + 512 * (nn + 1)],
                                   kq == 0, kq == 3,
                                   inc=s_mm if (kq == 3 and m == 7)
                                   else None)
                        m_wo = s_mm.total
                        p.wait("vector", s_mm, m_wo)
                        for m in range(8):
                            p.run("vector",
                                  lambda e, m=m, c=c, nn=nn:
                                  e.tensor_add(
                                      xres[:, m * T + 1024 * c + 512 * nn:
                                           m * T + 1024 * c + 512 * (nn + 1)],
                                      xres[:, m * T + 1024 * c + 512 * nn:
                                           m * T + 1024 * c + 512 * (nn + 1)],
                                      pp[:, m, :]), inc=s_ve)
                            if q == 3 and c == 1 and nn == 1:
                                m5_marks.append(s_ve.total)
                        p.wait("tensor", s_ve, s_ve.total)
                wo_mark = s_mm.total

        if stop == "x4":
            emit_ffn("f2", src_marks=m5_marks)

        p.check()

        p.wait("gpsimd", s_ve, s_ve.total)
        p.wait("gpsimd", s_sc, s_sc.total)
        dma("gpsimd", s_st, dout.rearrange("(k p) t -> p k t", p=128),
            xres[:].rearrange("p (k t) -> p k t", k=8))
        p.wait("gpsimd", s_st, s_st.total)
        p.check()

        for s in p.sems.values():
            s.h = stk.enter_context(nc.semaphore(s.name))
        block = stk.enter_context(nc.Block())

        for eng in Prog.ENGINES:
            def make(eng):
                def body(e):
                    for f in p.ops[eng]:
                        f(e)
                return body
            getattr(block, eng)(make(eng))
    finally:
        stk.close()
    return nc


def _sel_np():
    s = np.zeros((16, 2048), dtype=np.float32)
    for j in range(16):
        s[j, 128 * j:128 * (j + 1)] = 1.0
    return s


def kernel(**inputs):
    d = inputs
    x = np.ascontiguousarray(d["L0_attn_a_x"]).astype(np.float32, copy=False)
    base = {
        "wqkvT": np.ascontiguousarray(d["L0_attn_a_qkv_w"].T),
        "wprojT": np.ascontiguousarray(d["L0_attn_a_proj_w"].T),
        "qkvb": np.ascontiguousarray(d["L0_attn_a_qkv_b"]),
        "projb": np.ascontiguousarray(d["L0_attn_a_proj_b"]),
        "nw": np.ascontiguousarray(d["L0_attn_a_norm_w"]),
        "nb": np.ascontiguousarray(d["L0_attn_a_norm_b"]),
        "wgT": np.ascontiguousarray(d["L0_ffn_w_gate"].T),
        "wuT": np.ascontiguousarray(d["L0_ffn_w_up"].T),
        "wdT": np.ascontiguousarray(d["L0_ffn_w_down"].T),
        "winT": np.ascontiguousarray(d["L0_attn_b_W_in"].T),
        "woutT": np.ascontiguousarray(d["L0_attn_b_W_out"].T),
        "wxT": np.ascontiguousarray(d["L0_attn_b_W_xproj"].T),
        "wdtT": np.ascontiguousarray(d["L0_attn_b_W_dt"].T),
        "alog": np.ascontiguousarray(d["L0_attn_b_A_log"]),
        "wconv": np.ascontiguousarray(d["L0_attn_b_W_conv"][:, 0, :]),
        "bconv": np.ascontiguousarray(d["L0_attn_b_b_conv"]),
        "bdt": np.ascontiguousarray(d["L0_attn_b_b_dt"]),
        "dvec": np.ascontiguousarray(d["L0_attn_b_D"]),
        "cvals": np.tile(np.asarray(CVALS, dtype=np.float32), (128, 1)),
        "onesin": np.ones((128, 128), dtype=np.float32),
        "selin": _sel_np(),
    }
    nc = build_nc()
    in_maps = []
    for i in range(8):
        m = dict(base)
        m["xT"] = np.ascontiguousarray(x[i].T)
        in_maps.append(m)
    res = run_bass_kernel_spmd(nc, in_maps, core_ids=list(range(8)))
    out = np.stack([np.asarray(r["outT"]).T for r in res.results])
    return out.astype(np.float32)


if __name__ == "__main__":
    pass


# version 17
# speedup vs baseline: 1.0261x; 1.0126x over previous
import numpy as np
from contextlib import ExitStack

import concourse.bass as bass
import concourse.mybir as mybir
from concourse.bass_utils import run_bass_kernel_spmd

F32 = mybir.dt.float32
F32R = mybir.dt.float32r
I32 = mybir.dt.int32
ALU = mybir.AluOpType
ACT = mybir.ActivationFunctionType

C = 1024
T = 2048
HID = 3072
DI = 2048
DS = 16
DTR = 64
CT = C // 128
EPS = 1e-5
S127 = float(np.float32(1.0 / 127.0))
LNS127 = float(np.log(np.float64(1.0) / 127.0))
ISQ = 1.0 / 32.0
NTC = 4
ARN = 33280

STOP = "x4"
CVALS = [EPS, 1.0, LNS127, -0.5, 1.0 / 1024, S127, ISQ, 127.0]



def _full_tile(tile_fn, k):
    a0 = tile_fn(k, 0)
    t = a0.tensor.ap()
    off = a0.offset
    return t[:, off:off + 2048]



class Sem:
    __slots__ = ("name", "total", "h")

    def __init__(self, name):
        self.name = name
        self.total = 0
        self.h = None

    def inc(self, n=1):
        self.total += n
        return self.total


class Prog:
    ENGINES = ("sync", "gpsimd", "tensor", "vector", "scalar")

    def __init__(self):
        self.ops = {e: [] for e in self.ENGINES}
        self.sems = {}
        self.waits = []

    def sem(self, name):
        if name not in self.sems:
            self.sems[name] = Sem(name)
        return self.sems[name]

    def wait(self, eng, sem, val):
        if val <= 0:
            return
        self.waits.append((sem, val))
        self.ops[eng].append(lambda e, s=sem, v=val: e.wait_ge(s.h, v))

    def run(self, eng, build, inc=None, incn=1):
        if inc is not None:
            inc.inc(incn)

        def fn(e, b=build, s=inc, n=incn):
            i = b(e)
            if s is not None:
                i.then_inc(s.h, n)

        self.ops[eng].append(fn)
        return inc.total if inc is not None else None

    def check(self):
        for s, v in self.waits:
            assert v <= s.total, f"wait {s.name}>={v} but total={s.total}"

    def barrier(self):
        snap = [(s, s.total) for s in self.sems.values() if s.total > 0]
        for eng in self.ENGINES:
            for s, v in snap:
                self.wait(eng, s, v)


def build_nc(stop=STOP):
    p = Prog()
    nc = bass.Bass("TRN2", target_bir_lowering=False)

    dx = nc.dram_tensor("xT", [C, T], F32R, kind="ExternalInput").ap()
    dqkv = nc.dram_tensor("wqkvT", [C, 3 * C], F32R, kind="ExternalInput").ap()
    dproj = nc.dram_tensor("wprojT", [C, C], F32R, kind="ExternalInput").ap()
    dqkvb = nc.dram_tensor("qkvb", [3 * C], F32R, kind="ExternalInput").ap()
    dprojb = nc.dram_tensor("projb", [C], F32R, kind="ExternalInput").ap()
    dnw = nc.dram_tensor("nw", [C], F32R, kind="ExternalInput").ap()
    dnb = nc.dram_tensor("nb", [C], F32R, kind="ExternalInput").ap()
    dwg = nc.dram_tensor("wgT", [C, HID], I32, kind="ExternalInput").ap()
    dwu = nc.dram_tensor("wuT", [C, HID], I32, kind="ExternalInput").ap()
    dwd = nc.dram_tensor("wdT", [HID, C], I32, kind="ExternalInput").ap()
    dwin = nc.dram_tensor("winT", [C, 2 * DI], F32R, kind="ExternalInput").ap()
    dwout = nc.dram_tensor("woutT", [DI, C], F32R, kind="ExternalInput").ap()
    dwx = nc.dram_tensor("wxT", [DI, 96], F32R, kind="ExternalInput").ap()
    dwdt = nc.dram_tensor("wdtT", [DTR, DI], F32R, kind="ExternalInput").ap()
    dalog = nc.dram_tensor("alog", [DI, DS], F32R, kind="ExternalInput").ap()
    dwc = nc.dram_tensor("wconv", [DI, 4], F32R, kind="ExternalInput").ap()
    dbc = nc.dram_tensor("bconv", [DI], F32R, kind="ExternalInput").ap()
    dbdt = nc.dram_tensor("bdt", [DI], F32R, kind="ExternalInput").ap()
    ddv = nc.dram_tensor("dvec", [DI], F32R, kind="ExternalInput").ap()
    dcv = nc.dram_tensor("cvals", [128, len(CVALS)], F32,
                         kind="ExternalInput").ap()
    don = nc.dram_tensor("onesin", [128, 128], F32R,
                         kind="ExternalInput").ap()
    dsel = nc.dram_tensor("selin", [16, 2048], F32R,
                          kind="ExternalInput").ap()
    dout = nc.dram_tensor("outT", [C, T], F32R, kind="ExternalOutput").ap()
    sc_h = nc.dram_tensor("sc_h", [C, T], F32R, kind="Internal").ap()
    sc_kv = nc.dram_tensor("sc_kv", [C, C], F32R, kind="Internal").ap()
    sc_v = nc.dram_tensor("sc_v", [T, C], F32R, kind="Internal").ap()
    sc_x = nc.dram_tensor("sc_x", [C, T], F32R, kind="Internal").ap()
    sc_res = nc.dram_tensor("sc_res", [DI, T], F32R, kind="Internal").ap()
    sc_u = nc.dram_tensor("sc_u", [DI, T], F32R, kind="Internal").ap()

    stk = ExitStack()
    try:
        xres = stk.enter_context(nc.sbuf_tensor("xres", [128, 8 * T], F32R)).ap()
        ar = stk.enter_context(nc.sbuf_tensor("ar", [128, ARN], F32R)).ap()
        consts = stk.enter_context(nc.sbuf_tensor("consts", [128, 432], F32)).ap()
        wxc = stk.enter_context(nc.sbuf_tensor("wxc", [128, 768], F32R)).ap()
        sel = stk.enter_context(nc.sbuf_tensor("sel", [16, 2048], F32R)).ap()
        ones = stk.enter_context(nc.sbuf_tensor("ones", [128, 128], F32R)).ap()
        pp = stk.enter_context(nc.psum_tensor("pp", [128, 8, 512], F32)).ap()

        acol = consts[:, 0:256]
        wcv = consts[:, 256:320]
        bcv = consts[:, 320:336]
        bdt = consts[:, 336:352]
        dvc = consts[:, 352:368]
        nwc = consts[:, 368:376]
        nbc = consts[:, 376:384]
        qbc = consts[:, 384:408]
        pbc = consts[:, 408:416]
        wxsl2 = [wxc[:, 0:384], wxc[:, 384:768]]

        state = ar[:, 18444:18508]
        da_gate = [0, 0]

        sems = {n: p.sem(n) for n in
                ("s_in", "s_w", "s_g", "s_mm", "s_sc", "s_ve", "s_st")}
        s_in, s_w, s_g, s_mm, s_sc, s_ve, s_st = (
            sems["s_in"], sems["s_w"], sems["s_g"], sems["s_mm"],
            sems["s_sc"], sems["s_ve"], sems["s_st"])

        def dma(eng, sem, dst, src, slow=False):
            p.run(eng,
                  lambda e, d=dst, s=src, sl=slow: e.dma_start(
                      out=d, in_=s, allow_slow_non_contiguous=sl),
                  inc=sem, incn=16)
            return sem.total

        def mm(bank_ap, lhsT, rhs, start, stop, inc=None):
            p.run("tensor",
                  lambda e, o=bank_ap, l=lhsT, r=rhs, st=start, sp=stop:
                  e.matmul(o, l, r, start=st, stop=sp), inc=inc)
            return inc.total if inc else None

        def sact(out, in_, func, bias=0.0, scale=1.0, inc=None):
            p.run("scalar",
                  lambda e, o=out, i=in_, f=func, b=bias, s=scale:
                  e.activation(o, i, f, bias=b, scale=s), inc=inc or s_sc)
            return (inc or s_sc).total

        for ci, cv in enumerate(CVALS):
            nc.const_aps.register(F32, cv, consts[:, 416 + ci: 417 + ci])
        dma("sync", p.sem("s_in"), consts[:, 416:416 + len(CVALS)], dcv)

        dma("sync", s_in, xres[:].rearrange("p (k t) -> p k t", k=8),
            dx.rearrange("(k p) t -> p k t", p=128))
        dma("sync", s_in, acol.rearrange("p (g n) -> p g n", g=16),
            dalog.bitcast(F32).rearrange("(g p) n -> p g n", p=128))
        dma("sync", s_in, wcv.rearrange("p (g j) -> p g j", g=16),
            dwc.bitcast(F32).rearrange("(g p) j -> p g j", p=128))
        dma("sync", s_in, bcv, dbc.bitcast(F32).rearrange("(g p) -> p g", p=128),
            slow=True)
        dma("sync", s_in, bdt, dbdt.bitcast(F32).rearrange("(g p) -> p g", p=128),
            slow=True)
        dma("sync", s_in, dvc, ddv.bitcast(F32).rearrange("(g p) -> p g", p=128),
            slow=True)
        dma("sync", s_in, nwc, dnw.bitcast(F32).rearrange("(g p) -> p g", p=128),
            slow=True)
        dma("sync", s_in, nbc, dnb.bitcast(F32).rearrange("(g p) -> p g", p=128),
            slow=True)
        dma("sync", s_in, qbc, dqkvb.bitcast(F32).rearrange("(g p) -> p g", p=128),
            slow=True)
        dma("sync", s_in, pbc, dprojb.bitcast(F32).rearrange("(g p) -> p g", p=128),
            slow=True)
        dma("sync", s_in, ones[:], don)
        dma("sync", s_in, sel[:], dsel)
        v_in = s_in.total
        c127 = consts[:, 423:424]
        s127col = consts[:, 424:425]
        p.wait("vector", s_in, s_in.total)
        p.run("vector", lambda e: e.reciprocal(s127col, c127), inc=s_ve)
        p.wait("scalar", s_in, v_in)
        sact(acol, acol, ACT.Exp)
        p.wait("vector", s_sc, s_sc.total)
        p.run("vector",
              lambda e: e.tensor_scalar(acol, acol, -1.0, None, ALU.mult),
              inc=s_ve)
        p.barrier()

        def xres_t(k, tc):
            return xres[:, k * T + tc * 512: k * T + (tc + 1) * 512]

        LW = 19968
        SQ = [LW, LW + 2048]
        MUR = LW + 4096
        VRR = LW + 6144
        MUB = LW + 8192
        RRB = LW + 10240

        def _attn_post(k):
            p.wait("vector", s_ve, s_ve.total)
            p.run("vector",
                  lambda e, k=k: e.tensor_add(
                      xres[:, k * T:(k + 1) * T], xres[:, k * T:(k + 1) * T],
                      ar[:, AH + k * T: AH + (k + 1) * T]), inc=s_ve)

        sq_last = [0]
        ln_int = {}

        def emit_ln(src_tile, dst_tile, ntiles=CT, rr_bias=0.0, affine=None,
                    pre_spill=None, post_tile=None, src_marks=None,
                    scale_col=None, scratch=None, chunk_major=False,
                    bank_gate_ve=None):
            if scratch is None:
                sq0, sq1, muro, vrro, mubo, rrbo = (SQ[0], SQ[1], MUR, VRR,
                                                    MUB, RRB)
            else:
                sq0, sq1, muro, vrro, mubo, rrbo = scratch
            sqo = [sq0, sq1]
            inv = 1.0 / (128 * ntiles)
            if pre_spill is not None:
                spill_info = pre_spill()
            else:
                spill_info = None
            p.wait("tensor", s_sc, s_sc.total)
            if bank_gate_ve is not None:
                p.wait("tensor", s_ve, bank_gate_ve)
            sq_gate = [sq_last[0], sq_last[0]]
            for k in range(ntiles):
                if src_marks is not None:
                    p.wait("tensor", s_ve, src_marks[k])
                for nc_ in range(4):
                    mm(pp[0:1, 4 + nc_, :], ones[:, 0:1],
                       src_tile(k, nc_), k == 0, k == ntiles - 1,
                       inc=s_mm if (k == ntiles - 1 and nc_ == 3) else None)
            m_s1 = s_mm.total
            for k in range(ntiles):
                par = k % 2
                sq = ar[:, sqo[par]:sqo[par] + 2048]
                p.wait("scalar", s_mm, sq_gate[par])
                if src_marks is not None:
                    p.wait("scalar", s_ve, src_marks[k])
                p.run("scalar",
                      lambda e, sq=sq, st=_full_tile(src_tile, k):
                      e.activation(sq, st, ACT.Square), inc=s_sc)
                sc_sq = s_sc.total
                p.wait("tensor", s_sc, sc_sq)
                for nc_ in range(4):
                    mm(pp[0:1, nc_, :], ones[:, 0:1],
                       sq[:, 512 * nc_:512 * (nc_ + 1)],
                       k == 0, k == ntiles - 1,
                       inc=s_mm)
                sq_gate[par] = s_mm.total
            m_s2 = s_mm.total
            sq_last[0] = m_s2
            ln_int["m_s2"] = m_s2
            mu_r = ar[0:1, muro:muro + 2048]
            vr_r = ar[0:1, vrro:vrro + 2048]
            tmp_r = ar[0:1, sq0:sq0 + 2048]
            p.wait("scalar", s_mm, m_s1)
            for nc_ in range(4):
                sact(mu_r[:, 512 * nc_:512 * (nc_ + 1)], pp[0:1, 4 + nc_, :],
                     ACT.Copy, scale=inv)
            p.wait("scalar", s_mm, m_s2)
            for nc_ in range(4):
                sact(vr_r[:, 512 * nc_:512 * (nc_ + 1)], pp[0:1, nc_, :],
                     ACT.Copy, scale=inv)
            sc_rows = s_sc.total
            p.wait("vector", s_sc, sc_rows)
            p.run("vector", lambda e: e.tensor_mul(tmp_r, mu_r, mu_r),
                  inc=s_ve)
            p.wait("vector", s_ve, s_ve.total)
            p.run("vector", lambda e: e.tensor_sub(vr_r, vr_r, tmp_r),
                  inc=s_ve)
            p.wait("scalar", s_ve, s_ve.total)
            sact(vr_r, vr_r, ACT.Ln, bias=EPS)
            p.wait("scalar", s_sc, s_sc.total)
            sact(vr_r, vr_r, ACT.Exp, scale=-0.5, bias=rr_bias)
            sc_rr = s_sc.total
            ln_int["sc_rr"] = sc_rr
            if scale_col is not None:
                p.wait("vector", s_sc, sc_rr)
                p.run("vector",
                      lambda e: e.tensor_scalar(vr_r, vr_r, scale_col[0:1, :],
                                                None, ALU.mult), inc=s_ve)
                sc_rr_v = s_ve.total
                p.wait("tensor", s_ve, sc_rr_v)
            p.wait("tensor", s_sc, sc_rr)
            for nc_ in range(4):
                mm(pp[:, nc_, :], ones[0:1, :],
                   mu_r[:, 512 * nc_:512 * (nc_ + 1)], True, True,
                   inc=s_mm)
                mm(pp[:, 4 + nc_, :], ones[0:1, :],
                   vr_r[:, 512 * nc_:512 * (nc_ + 1)], True, True,
                   inc=s_mm)
            m_bc = s_mm.total
            ln_int["m_bc"] = m_bc
            p.wait("scalar", s_mm, m_bc)
            for nc_ in range(4):
                sact(ar[:, mubo + 512 * nc_:mubo + 512 * (nc_ + 1)],
                     pp[:, nc_, :], ACT.Copy)
                sact(ar[:, rrbo + 512 * nc_:rrbo + 512 * (nc_ + 1)],
                     pp[:, 4 + nc_, :], ACT.Copy)
            sc_bc = s_sc.total
            ln_int["sc_bc"] = sc_bc
            p.wait("vector", s_sc, sc_bc)
            if spill_info is not None:
                p.wait("vector", spill_info[0], spill_info[1])
            mub = ar[:, mubo:mubo + 2048]
            rrb = ar[:, rrbo:rrbo + 2048]
            if chunk_major:
                dst_marks = []
                for tcx in range(4):
                    mubc = ar[:, mubo + 512 * tcx: mubo + 512 * (tcx + 1)]
                    rrbc = ar[:, rrbo + 512 * tcx: rrbo + 512 * (tcx + 1)]
                    for k in range(ntiles):
                        dst = dst_tile(k, tcx)
                        srcc = src_tile(k, tcx)
                        p.run("vector",
                              lambda e, dst=dst, srcc=srcc, mubc=mubc:
                              e.tensor_sub(dst, srcc, mubc), inc=s_ve)
                        p.wait("vector", s_ve, s_ve.total)
                        p.run("vector",
                              lambda e, dst=dst, rrbc=rrbc:
                              e.tensor_mul(dst, dst, rrbc), inc=s_ve)
                    dst_marks.append(s_ve.total)
                p.wait("scalar", s_ve, s_ve.total)
                return dst_marks
            dst_marks = []
            for k in range(ntiles):
                dst = _full_tile(dst_tile, k)
                srcf = _full_tile(src_tile, k)
                p.run("vector",
                      lambda e, dst=dst, srcf=srcf: e.tensor_sub(
                          dst, srcf, mub), inc=s_ve)
                p.wait("vector", s_ve, s_ve.total)
                p.run("vector",
                      lambda e, dst=dst: e.tensor_mul(dst, dst, rrb),
                      inc=s_ve)
                if affine is not None:
                    wf, bf = affine
                    p.wait("vector", s_ve, s_ve.total)
                    p.run("vector",
                          lambda e, dst=dst, k=k: e.tensor_scalar(
                              dst, dst, wf(k), bf(k), ALU.mult, ALU.add),
                          inc=s_ve)
                if post_tile is not None:
                    post_tile(k)
                dst_marks.append(s_ve.total)
            p.wait("scalar", s_ve, s_ve.total)
            return dst_marks

        AH = 3584
        def y_t(k, tc):
            return ar[:, AH + k * T + tc * 512: AH + k * T + (tc + 1) * 512]

        marks1 = emit_ln(xres_t, y_t, post_tile=_attn_post)
        marks2 = emit_ln(y_t, y_t, affine=(lambda k: nwc[:, k:k + 1],
                                           lambda k: nbc[:, k:k + 1]),
                         src_marks=marks1)
        ln2i = dict(ln_int)
        s_sp = p.sem("s_sp")
        for k in range(CT):
            p.wait("gpsimd", s_ve, marks2[k])
            dma("gpsimd", s_sp, sc_h[128 * k:128 * (k + 1), :],
                ar[:, AH + k * T: AH + (k + 1) * T])
        g_h = s_sp.total
        qkvrow = ar[0:1, 19968:23040]
        p.wait("sync", s_mm, ln2i["m_s2"])
        p.wait("sync", s_sc, ln2i["sc_rr"])
        dma("sync", s_w, qkvrow, dqkvb[None, :])
        dma("sync", s_w, ar[:, 23040:24064], dqkv[0:128, C:2 * C])
        p.wait("sync", s_mm, ln2i["m_bc"])
        for k in range(1, 5):
            dma("sync", s_w, ar[:, 23040 + k * 1024: 23040 + (k + 1) * 1024],
                dqkv[128 * k:128 * (k + 1), C:2 * C])
        p.wait("sync", s_ve, s_ve.total)
        for k in range(5, CT):
            dma("sync", s_w, ar[:, 23040 + k * 1024: 23040 + (k + 1) * 1024],
                dqkv[128 * k:128 * (k + 1), C:2 * C])
        w_wk = s_w.total
        p.wait("tensor", s_w, w_wk)
        p.wait("tensor", s_sc, ln2i["sc_bc"])

        def h_lhs(k, i):
            return ar[:, AH + k * T + 128 * i: AH + k * T + 128 * (i + 1)]

        def k_sl(i, cg):
            return ar[:, AH + cg * T + 128 * i: AH + cg * T + 128 * (i + 1)]

        p.wait("scalar", p.sem("s_sp"), g_h)
        ev_gate = {}
        for i in range(16):
            if i >= 4:
                p.wait("tensor", s_sc, ev_gate[i - 4])
            marks = []
            for oc in range(2):
                bank = 2 * (i % 4) + oc
                for k in range(CT):
                    if i == 0 and oc == 0:
                        p.wait("tensor", s_ve, marks2[k])
                    mm(pp[:, bank, :], h_lhs(k, i),
                       ar[:, 23040 + k * 1024 + 512 * oc:
                          23040 + k * 1024 + 512 * (oc + 1)],
                       k == 0, False)
                mm(pp[:, bank, :], ones[0:1, 0:128],
                   qkvrow[:, C + 512 * oc: C + 512 * (oc + 1)],
                   False, True, inc=s_mm)
                marks.append(s_mm.total)
            bank = 2 * (i % 4)
            p.wait("scalar", s_mm, marks[1])
            out_ap = (ar[:, AH:AH + 8 * T]
                      .rearrange("p (a t) -> p a t", a=8)
                      [:, :, 128 * i:128 * (i + 1)])
            in_ap = pp[:, bank:bank + 2, :].rearrange(
                "p b (a c) -> p (b a) c", a=4)
            p.run("scalar",
                  lambda e, o=out_ap, s=in_ap: e.activation(
                      o, s, ACT.Copy), inc=s_sc)
            ev_gate[i] = s_sc.total
        p.barrier()

        p.wait("gpsimd", p.sem("s_sp"), g_h)
        p.wait("sync", s_mm, s_mm.total)
        for k in range(CT):
            dma("sync", s_w, ar[:, 23040 + k * 1024: 23040 + (k + 1) * 1024],
                dqkv[128 * k:128 * (k + 1), 2 * C:3 * C])
        w_wv = s_w.total
        vstg2 = [ar[:, 0:1024], ar[:, 2048:3072]]
        vspill_marks = {}
        HSL = [31232, 32256, 1024]
        vst_gate = [0, 0]
        hv_gate = {}
        vev_gate = {}
        for i in range(16):
            s_h2 = p.sem(f"s_h{i % 3}")
            hs = ar[:, HSL[i % 3]:HSL[i % 3] + 1024]
            hv = hs.rearrange("p (k b) -> p k b", k=8)
            if i >= 4:
                p.wait("tensor", s_sc, vev_gate[i - 4])
            if i >= 3:
                p.wait("sync", s_mm, hv_gate[i - 3])
            dma("sync", s_h2, hv,
                sc_h.rearrange("(k p) t -> p k t", p=128)
                [:, :, 128 * i:128 * (i + 1)])
            g_hi = s_h2.total
            p.wait("tensor", s_h2, g_hi)
            if i == 0:
                p.wait("tensor", s_w, w_wv)
            marks = []
            for oc in range(2):
                bank = 2 * (i % 4) + oc
                for k in range(CT):
                    mm(pp[:, bank, :], hv[:, k, :],
                       ar[:, 23040 + k * 1024 + 512 * oc:
                          23040 + k * 1024 + 512 * (oc + 1)],
                       k == 0, False)
                mm(pp[:, bank, :], ones[0:1, 0:128],
                   qkvrow[:, 2 * C + 512 * oc: 2 * C + 512 * (oc + 1)],
                   False, True, inc=s_mm)
                marks.append(s_mm.total)
            hv_gate[i] = marks[1]
            bank = 2 * (i % 4)
            j = i % 2
            s_vo = p.sem(f"s_vo{j}")
            p.wait("scalar", s_mm, marks[1])
            if vst_gate[j]:
                p.wait("scalar", s_vo, vst_gate[j])
            p.run("scalar",
                  lambda e, j=j, bank=bank: e.activation(
                      vstg2[j], pp[:, bank:bank + 2, :], ACT.Copy),
                  inc=s_sc)
            sc_v1 = s_sc.total
            p.wait("gpsimd", s_sc, sc_v1)
            vst_gate[j] = dma(
                "gpsimd", s_vo,
                sc_v[128 * i:128 * (i + 1), :], vstg2[j])
            vev_gate[i] = s_sc.total
            vspill_marks[i] = (p.sem(f"s_vo{i % 2}"), vst_gate[i % 2])

        kvstg = [ar[:, 2560:3072], ar[:, 3072:3584]]
        vld = [ar[:, 0:512], ar[:, 512:1024], ar[:, 1024:1536],
               ar[:, 1536:2048], ar[:, 2048:2560]]
        kv_out_gate = {}
        kv_g = 0
        kv_ev_mark = s_sc.total
        for ch in range(2):
            p.wait("tensor", s_sc, kv_ev_mark)
            vld_gate = {}
            round_start = s_mm.total
            for i in range(16):
                s_vl = p.sem(f"s_vl{i % 5}")
                gate = vld_gate[i - 5] if i >= 5 else (
                    0 if ch == 0 else round_start)
                p.wait("sync", s_mm, gate)
                if ch == 0:
                    sm, sv = vspill_marks[i]
                    p.wait("sync", sm, sv)
                    if i % 5 in (0, 1):
                        sm2, sv2 = vspill_marks[14]
                        p.wait("sync", sm2, sv2)
                    if i % 5 == 4:
                        sm2, sv2 = vspill_marks[15]
                        p.wait("sync", sm2, sv2)
                dma("sync", s_vl, vld[i % 5],
                    sc_v[128 * i:128 * (i + 1), 512 * ch:512 * (ch + 1)])
                g_vi = s_vl.total
                p.wait("tensor", s_vl, g_vi)
                for cc in range(8):
                    mm(pp[:, cc, :], k_sl(i, cc), vld[i % 5],
                       i == 0, i == 15,
                       inc=s_mm if cc == 7 else None)
                vld_gate[i] = s_mm.total
            m_kv = s_mm.total
            p.wait("scalar", s_mm, m_kv)
            for cc in range(8):
                ks = kvstg[kv_g % 2]
                s_kv2 = p.sem(f"s_kv{kv_g % 2}")
                if kv_g >= 2:
                    p.wait("scalar", s_kv2, kv_out_gate[kv_g - 2])
                sact(ks, pp[:, cc, :], ACT.Copy, scale=ISQ)
                sc_e = s_sc.total
                p.wait("gpsimd", s_sc, sc_e)
                kv_out_gate[kv_g] = dma(
                    "gpsimd", s_kv2,
                    sc_kv[128 * cc:128 * (cc + 1), 512 * ch:512 * (ch + 1)],
                    ks)
                kv_g += 1
            kv_ev_mark = s_sc.total
        p.barrier()

        kvb = 3584
        s_kvld = p.sem("s_kvld")
        dma("gpsimd", s_kvld,
            ar[:, kvb:kvb + 8192].rearrange("p (k t) -> p k t", k=8),
            sc_kv.rearrange("(k p) t -> p k t", p=128))
        g_kv = s_kvld.total

        def kv_sl(k, m):
            return ar[:, kvb + k * 1024 + 128 * m: kvb + k * 1024 + 128 * (m + 1)]

        qb, ob, wb = 19968, 24064, 28160
        m_q_hist = {}
        prev_proj_marks = None
        b4_marks = []
        for tc in range(NTC):
            hb = 11776 + 4096 * (tc % 2)
            s_hc = p.sem(f"s_hc{tc % 2}")
            if tc >= 2:
                p.wait("gpsimd", s_mm, m_q_hist[tc - 2])
            dma("gpsimd", s_hc,
                ar[:, hb:hb + 4096].rearrange("p (k t) -> p k t", k=8),
                sc_h.rearrange("(k p) t -> p k t", p=128)
                [:, :, 512 * tc:512 * (tc + 1)])
            g_hc = s_hc.total

            def h_c(k, hb=hb):
                return ar[:, hb + 512 * k: hb + 512 * (k + 1)]

            kmarks = []
            wq_loaded = {}

            def load_wq(k):
                wsl_ = ar[:, wb + 1024 * (k % 2): wb + 1024 * (k % 2) + 1024]
                s_w2_ = p.sem(f"s_w{k % 2}")
                if k >= 2:
                    p.wait("sync", s_mm, kmarks[k - 2])
                elif prev_proj_marks is not None:
                    p.wait("sync", s_mm, prev_proj_marks[k + 6])
                dma("sync", s_w2_, wsl_, dqkv[128 * k:128 * (k + 1), 0:C])
                wq_loaded[k] = s_w2_.total

            load_wq(0)
            load_wq(1)
            for k in range(CT):
                wsl = ar[:, wb + 1024 * (k % 2): wb + 1024 * (k % 2) + 1024]
                s_w2 = p.sem(f"s_w{k % 2}")
                p.wait("tensor", s_w2, wq_loaded[k])
                if k == 0:
                    p.wait("tensor", s_hc, g_hc)
                    p.wait("tensor", p.sem("s_kvld"), g_kv)
                for m in range(8):
                    mm(pp[:, m, :], wsl[:, 128 * m:128 * (m + 1)], h_c(k),
                       k == 0, k == CT - 1,
                       inc=s_mm if m == 7 else None)
                kmarks.append(s_mm.total)
                if k + 2 < CT:
                    load_wq(k + 2)
            m_q = s_mm.total
            m_q_hist[tc] = m_q
            p.wait("vector", s_mm, m_q)
            q_ev = []
            for m in range(8):
                p.run("vector",
                      lambda e, m=m: e.tensor_scalar(
                          ar[:, qb + 512 * m: qb + 512 * (m + 1)],
                          pp[:, m, :], qbc[:, m:m + 1], None, ALU.add),
                      inc=s_ve)
                q_ev.append(s_ve.total)
            sc_q = s_ve.total
            for m in range(8):
                p.wait("tensor", s_ve, q_ev[m])
                for k in range(CT):
                    p.wait("tensor", s_ve, q_ev[k]) if k > m else None
                    mm(pp[:, m, :], kv_sl(k, m),
                       ar[:, qb + 512 * k: qb + 512 * (k + 1)],
                       k == 0, k == CT - 1,
                       inc=s_mm if k == CT - 1 else None)
            m_o = s_mm.total
            p.wait("scalar", s_mm, m_o)
            for m in range(8):
                sact(ar[:, ob + 512 * m: ob + 512 * (m + 1)], pp[:, m, :],
                     ACT.Copy)
            sc_o = s_sc.total
            p.wait("tensor", s_sc, sc_o)
            kmarks2 = []
            wp_loaded = {}

            def load_wp(k):
                wsl_ = ar[:, wb + 1024 * (k % 2): wb + 1024 * (k % 2) + 1024]
                s_w2_ = p.sem(f"s_w{k % 2}")
                p.wait("sync", s_mm,
                       kmarks2[k - 2] if k >= 2 else kmarks[k + 6])
                dma("sync", s_w2_, wsl_, dproj[128 * k:128 * (k + 1), :])
                wp_loaded[k] = s_w2_.total

            load_wp(0)
            load_wp(1)
            for k in range(CT):
                wsl = ar[:, wb + 1024 * (k % 2): wb + 1024 * (k % 2) + 1024]
                s_w2 = p.sem(f"s_w{k % 2}")
                p.wait("tensor", s_w2, wp_loaded[k])
                for m in range(8):
                    mm(pp[:, m, :], wsl[:, 128 * m:128 * (m + 1)],
                       ar[:, ob + 512 * k: ob + 512 * (k + 1)],
                       k == 0, k == CT - 1,
                       inc=s_mm if m == 7 else None)
                kmarks2.append(s_mm.total)
                if k + 2 < CT:
                    load_wp(k + 2)
            prev_proj_marks = kmarks2
            m_p = s_mm.total
            p.wait("vector", s_mm, m_p)
            for m in range(8):
                p.run("vector",
                      lambda e, m=m, tc=tc: e.scalar_tensor_tensor(
                          xres_t(m, tc), pp[:, m, :], pbc[:, m:m + 1],
                          xres_t(m, tc), ALU.add, ALU.add), inc=s_ve)
                if tc == NTC - 1:
                    b4_marks.append(s_ve.total)
            p.wait("tensor", s_ve, s_ve.total)

        FSCR = (3584, 5632, 7680, 9728, 11776, 13824)

        def emit_ffn(tag, src_marks=None):
            DACC, WGB, WUB, HHB, WDB, XRB = (3584, 19968, 24064, 28160,
                                             30208, 0)
            ent_mm = s_mm.total
            ent_ve = s_ve.total
            ent_sc = s_sc.total

            def spill():
                s_sp2 = p.sem("s_fx0")
                p.wait("gpsimd", s_ve, s_ve.total)
                v = dma("gpsimd", s_sp2,
                        sc_x.rearrange("(k p) t -> p k t", p=128),
                        xres.rearrange("p (k t) -> p k t", k=8))
                return (s_sp2, v)

            ln_marks = emit_ln(
                xres_t, xres_t, scale_col=s127col, pre_spill=spill,
                src_marks=src_marks, scratch=FSCR, chunk_major=True,
                bank_gate_ve=(src_marks[-1] if src_marks else None))
            ln_all = ln_marks[-1]

            pairs = [(0, 1), (2, 3), (4, 5)]
            HH2 = [HHB, 0]
            WD4 = [WDB, WDB + 1024, 32256, 2048]
            s_wd4 = p.sem(f"s_wd4{tag}")
            wd_loaded = [0]
            stage_j = [0]
            stage_evac = {}
            hh_done = {}
            hh_rd_mm = {}
            hh_wr_ve = {}
            qprev = [None]
            dacc_sc_gate = [0]
            dn_stt_prev = [0]
            first = True
            wg_gate = 0
            down_mark_c = [0]

            def wait_ev(eng, ev):
                if ev is not None:
                    p.wait(eng, s_sc if ev[0] == "sc" else s_ve, ev[1])

            def stage(kind, tc, rnd, hhb):
                j = stage_j[0]
                stage_j[0] += 1
                pr = pairs[j % 3]
                wait_ev("tensor", stage_evac.get(j - 3))
                wb = WGB if kind == "g" else WUB
                mts = (2 * rnd, 2 * rnd + 1)
                for k in range(CT):
                    for mi, mt in enumerate(mts):
                        mm(pp[:, pr[mi], :],
                           ar[:, wb + 512 * k + 128 * mt:
                              wb + 512 * k + 128 * (mt + 1)],
                           xres_t(k, tc), k == 0, k == CT - 1,
                           inc=s_mm if (k == CT - 1 and mi == 1) else None)
                m_st = s_mm.total
                if kind == "g":
                    p.wait("scalar", s_mm, m_st)
                    if (tc % 2) in hh_rd_mm:
                        p.wait("scalar", s_mm, hh_rd_mm[tc % 2])
                    if (tc % 2) in hh_wr_ve:
                        p.wait("scalar", s_ve, hh_wr_ve[tc % 2])
                    for mi, mt in enumerate(mts):
                        sact(ar[:, hhb + 512 * mt: hhb + 512 * (mt + 1)],
                             pp[:, pr[mi], :], ACT.Silu)
                    stage_evac[j] = ("sc", s_sc.total)
                else:
                    p.wait("vector", s_mm, m_st)
                    p.wait("vector", s_sc, stage_evac[j - 1][1])
                    for mi, mt in enumerate(mts):
                        p.run("vector",
                              lambda e, mt=mt, mi=mi, hhb=hhb, pr=pr:
                              e.tensor_mul(
                                  ar[:, hhb + 512 * mt: hhb + 512 * (mt + 1)],
                                  ar[:, hhb + 512 * mt: hhb + 512 * (mt + 1)],
                                  pp[:, pr[mi], :]), inc=s_ve)
                    stage_evac[j] = ("ve", s_ve.total)

            def down_qtr(tc, hhb_prev, dq):
                nonlocal first
                p.wait("tensor", s_ve, hh_done[tc])
                p.wait("tensor", s_wd4, wd_loaded[0])
                wait_ev("tensor", qprev[0])
                ms_ = (2 * dq, 2 * dq + 1)
                for kd in range(4):
                    for mi, m in enumerate(ms_):
                        mm(pp[:, 6 + mi, :],
                           ar[:, WD4[kd] + 128 * m: WD4[kd] + 128 * (m + 1)],
                           ar[:, hhb_prev + 512 * kd:
                              hhb_prev + 512 * (kd + 1)],
                           kd == 0, kd == 3,
                           inc=s_mm if (kd == 3 and mi == 1) else None)
                m_q = s_mm.total
                hh_rd_mm[tc % 2] = m_q
                down_mark_c[0] = m_q
                if first:
                    p.wait("scalar", s_mm, m_q)
                    p.wait("scalar", s_ve, ln_all)
                    for mi, m in enumerate(ms_):
                        sact(ar[:, DACC + m * T + 512 * tc:
                                DACC + m * T + 512 * (tc + 1)],
                             pp[:, 6 + mi, :], ACT.Copy, scale=s127col)
                    dacc_sc_gate[0] = s_sc.total
                    qprev[0] = ("sc", s_sc.total)
                else:
                    p.wait("vector", s_mm, m_q)
                    p.wait("vector", s_sc, dacc_sc_gate[0])
                    p.wait("vector", s_ve, dn_stt_prev[0])
                    for mi, m in enumerate(ms_):
                        p.run("vector",
                              lambda e, m=m, mi=mi, tc=tc:
                              e.scalar_tensor_tensor(
                                  ar[:, DACC + m * T + 512 * tc:
                                     DACC + m * T + 512 * (tc + 1)],
                                  pp[:, 6 + mi, :], s127col,
                                  ar[:, DACC + m * T + 512 * tc:
                                     DACC + m * T + 512 * (tc + 1)],
                                  ALU.mult, ALU.add), inc=s_ve)
                    dn_stt_prev[0] = s_ve.total
                    qprev[0] = ("ve", s_ve.total)
            for mh in range(2):
                for ms in range(3):
                    s = 3 * mh + ms
                    s_wg = p.sem(f"s_fg{s % 2}")
                    s_wu = p.sem(f"s_fu{s % 2}")
                    p.wait("gpsimd", s_mm, wg_gate)
                    if s == 0:
                        p.wait("gpsimd", s_mm, ent_mm)
                        p.wait("gpsimd", s_ve, ent_ve)
                        p.wait("gpsimd", s_sc, ent_sc)
                    dma("gpsimd", s_wg,
                        ar[:, WGB:WGB + 4096].rearrange(
                            "p (k h) -> p k h", k=8),
                        dwg.rearrange("(k p) h -> p k h", p=128)
                        [:, :, 512 * s:512 * (s + 1)])
                    w_g = s_wg.total
                    dma("gpsimd", s_wu,
                        ar[:, WUB:WUB + 4096].rearrange(
                            "p (k h) -> p k h", k=8),
                        dwu.rearrange("(k p) h -> p k h", p=128)
                        [:, :, 512 * s:512 * (s + 1)])
                    w_u = s_wu.total
                    if s > 0:
                        p.wait("gpsimd", s_mm, down_mark_c[0])
                    for kd in range(4):
                        dma("gpsimd", s_wd4, ar[:, WD4[kd]:WD4[kd] + 1024],
                            dwd[128 * (4 * s + kd):
                                128 * (4 * s + kd + 1), :])
                    wd_loaded[0] = s_wd4.total
                    p.wait("tensor", s_wg, w_g)
                    p.wait("tensor", s_wu, w_u)
                    for tc in range(NTC):
                        hhb = HH2[tc % 2]
                        p.wait("tensor", s_ve, ln_marks[tc])
                        for rnd in range(2):
                            stage("g", tc, rnd, hhb)
                            if tc > 0:
                                down_qtr(tc - 1, HH2[(tc - 1) % 2], 2 * rnd)
                            stage("u", tc, rnd, hhb)
                            if tc > 0:
                                down_qtr(tc - 1, HH2[(tc - 1) % 2],
                                         2 * rnd + 1)
                        hh_done[tc] = s_ve.total
                        hh_wr_ve[tc % 2] = s_ve.total
                        if tc == NTC - 1:
                            wg_gate = s_mm.total
                    for dq in range(4):
                        down_qtr(NTC - 1, HH2[(NTC - 1) % 2], dq)
                    first = False

            p.barrier()
            add_marks = {}
            for tc in range(NTC):
                for k in range(CT):
                    j = (tc * CT + k) % 4
                    s_xr = p.sem(f"s_fr{j}")
                    idx = tc * CT + k
                    if idx >= 4:
                        p.wait("gpsimd", s_ve, add_marks[idx - 4])
                    xstg = ar[:, XRB + 512 * j: XRB + 512 * (j + 1)]
                    v = dma("gpsimd", s_xr, xstg,
                            sc_x[128 * k:128 * (k + 1),
                                 512 * tc:512 * (tc + 1)])
                    p.wait("vector", s_xr, v)
                    p.run("vector",
                          lambda e, k=k, tc=tc, xs=xstg: e.tensor_add(
                              xres_t(k, tc), xs,
                              ar[:, DACC + k * T + 512 * tc:
                                 DACC + k * T + 512 * (tc + 1)]),
                          inc=s_ve)
                    add_marks[idx] = s_ve.total
            return [add_marks[(NTC - 1) * CT + k] for k in range(CT)]

        f1_marks = None
        if stop != "x1":
            f1_marks = emit_ffn("f1", src_marks=b4_marks)

        if stop in ("x3", "x4"):
            M_XCP = 0
            M_UQ = 8204
            M_DL = 0
            M_DU = 4096
            M_Y = 8192
            M_BCB = 12288
            M_BCC = 13312
            M_DA = [14336, 15360]
            M_RC = 16396
            M_XD = 18688
            M_RB = 20736
            M_STG = 22784
            M_WSL = 24832
            M_WDT = 28928
            M_W = 30976
            M_H = 32000

            m_ent_mm = s_mm.total
            m_ent_ve = s_ve.total
            m_ent_sc = s_sc.total

            def spill_m():
                s_sp2 = p.sem("s_fx0")
                p.wait("gpsimd", s_ve, s_ve.total)
                v = dma("gpsimd", s_sp2,
                        sc_x.rearrange("(k p) t -> p k t", p=128),
                        xres.rearrange("p (k t) -> p k t", k=8))
                return (s_sp2, v)

            ln4_marks = emit_ln(xres_t, xres_t, pre_spill=spill_m,
                                src_marks=f1_marks, scratch=FSCR,
                                chunk_major=True,
                                bank_gate_ve=f1_marks[-1])
            ln4_sc = s_sc.total

            s_win = p.sem("s_win")
            s_wxc = p.sem("s_wxc")
            s_wdtl = p.sem("s_wdtl")
            s_uo = p.sem("s_uo")

            xp_marks = {}
            pend_xproj = []

            def _emit_xproj_body(q, sc_u_done, w_wx):
                p.wait("tensor", s_sc, s_sc.total)
                p.wait("tensor", w_wx[0], w_wx[1])
                wxv = wxsl2[q % 2].rearrange("p (a b) -> p a b", a=4)
                for nc_ in range(NTC):
                    bb = 3 * (nc_ % 2)
                    groups = [(64, 0, M_XD, pp[0:64, bb, :]),
                              (16, 64, M_RB, pp[0:16, bb + 1, :]),
                              (16, 80, M_RC, pp[0:16, bb + 2, :])]
                    for rows, coff, dst, pb in groups:
                        for kq in range(4):
                            mm(pb, wxv[:, kq, coff:coff + rows],
                               ar[:, M_UQ + 2048 * kq + 512 * nc_:
                                  M_UQ + 2048 * kq + 512 * (nc_ + 1)],
                               kq == 0, kq == 3,
                               inc=s_mm if kq == 3 else None)
                        m_xp = s_mm.total
                        if q == 0:
                            p.wait("scalar", s_mm, m_xp)
                            sact(ar[0:rows, dst + 512 * nc_:
                                    dst + 512 * (nc_ + 1)], pb, ACT.Copy)
                            p.wait("tensor", s_sc, s_sc.total)
                        else:
                            p.wait("vector", s_mm, m_xp)
                            p.run("vector",
                                  lambda e, rows=rows, dst=dst, nc_=nc_,
                                  pb=pb: e.tensor_add(
                                      ar[0:rows, dst + 512 * nc_:
                                         dst + 512 * (nc_ + 1)],
                                      ar[0:rows, dst + 512 * nc_:
                                         dst + 512 * (nc_ + 1)], pb),
                                  inc=s_ve)
                            p.wait("tensor", s_ve, s_ve.total)

            m1_mark = 0
            resout_gate = [0, 0, 0, 0]
            ev_rd = {}
            p.wait("sync", s_mm, m_ent_mm)
            p.wait("sync", s_ve, m_ent_ve)
            p.wait("sync", s_sc, m_ent_sc)
            for q in range(4):
                p.wait("sync", s_mm, m1_mark)
                for k in range(CT):
                    dma("sync", s_win,
                        ar[:, M_WSL + 1024 * k: M_WSL + 1024 * k + 512],
                        dwin[128 * k:128 * (k + 1), 512 * q:512 * (q + 1)])
                    dma("sync", s_win,
                        ar[:, M_WSL + 1024 * k + 512: M_WSL + 1024 * (k + 1)],
                        dwin[128 * k:128 * (k + 1),
                             2048 + 512 * q:2048 + 512 * (q + 1)])
                w_win = s_win.total
                p.wait("sync", s_mm, m1_mark)
                s_wxc2 = p.sem(f"s_wxc{q % 2}")
                if q >= 2:
                    p.wait("sync", s_mm, xp_marks[q - 2])
                for kq in range(4):
                    dma("sync", s_wxc2,
                        wxsl2[q % 2].rearrange("p (a b) -> p a b", a=4)
                        [:, kq, :],
                        dwx[512 * q + 128 * kq: 512 * q + 128 * (kq + 1), :])
                w_wx = (s_wxc2, s_wxc2.total)
                p.wait("tensor", s_win, w_win)
                p.wait("vector", s_ve, s_ve.total)
                for i in range(4):
                    p.run("vector",
                          lambda e, i=i: e.tensor_scalar(
                              ar[:, M_XCP + 2051 * i: M_XCP + 2051 * i + 3],
                              ones[:, 0:3], 0.0, None, ALU.mult), inc=s_ve)
                ve_pad = s_ve.total
                for nc_ in range(NTC):
                    for rd in range(2):
                        if nc_ == 2 and rd == 0 and pend_xproj:
                            pend_xproj.pop(0)()
                        prev = (q, nc_ - 1, rd) if nc_ > 0 else (q - 1, 3, rd)
                        p.wait("tensor", s_sc, ev_rd.get(prev, 0))
                        if q == 0:
                            p.wait("tensor", s_ve, ln4_marks[nc_])
                            if nc_ == 0 and rd == 0:
                                p.wait("tensor", s_sc, ln4_sc)
                        mts = (2 * rd, 2 * rd + 1)
                        for k in range(CT):
                            for mi, mt in enumerate(mts):
                                b0 = 4 * rd + 2 * mi
                                mm(pp[:, b0, :],
                                   ar[:, M_WSL + 1024 * k + 128 * mt:
                                      M_WSL + 1024 * k + 128 * (mt + 1)],
                                   xres_t(k, nc_), k == 0, k == CT - 1,
                                   inc=s_mm if (k == CT - 1 and mi == 1)
                                   else None)
                                mm(pp[:, b0 + 1, :],
                                   ar[:, M_WSL + 1024 * k + 512 + 128 * mt:
                                      M_WSL + 1024 * k + 512 + 128 * (mt + 1)],
                                   xres_t(k, nc_), k == 0, k == CT - 1,
                                   inc=s_mm if (k == CT - 1 and mi == 1)
                                   else None)
                        m_xr = s_mm.total
                        p.wait("scalar", s_mm, m_xr)
                        p.wait("scalar", s_ve, ve_pad)
                        for mi, mt in enumerate(mts):
                            b0 = 4 * rd + 2 * mi
                            sact(ar[:, M_XCP + 2051 * mt + 3 + 512 * nc_:
                                    M_XCP + 2051 * mt + 3 + 512 * (nc_ + 1)],
                                 pp[:, b0, :], ACT.Copy)
                            j = mt
                            s_ro = p.sem(f"s_ro{j}")
                            p.wait("scalar", s_ro, resout_gate[j])
                            sact(ar[:, M_STG + 512 * j: M_STG + 512 * (j + 1)],
                                 pp[:, b0 + 1, :], ACT.Silu)
                            sc_r = s_sc.total
                            p.wait("gpsimd", s_sc, sc_r)
                            resout_gate[j] = dma(
                                "gpsimd", s_ro,
                                sc_res[512 * q + 128 * mt:
                                       512 * q + 128 * (mt + 1),
                                       512 * nc_:512 * (nc_ + 1)],
                                ar[:, M_STG + 512 * j: M_STG + 512 * (j + 1)])
                        ev_rd[(q, nc_, rd)] = s_sc.total
                sc_after_evac = s_sc.total
                p.wait("vector", s_sc, sc_after_evac)
                p.wait("vector", s_ve, ve_pad)
                p.wait("vector", s_uo, s_uo.total)
                if q >= 1 and (q - 1) in xp_marks:
                    p.wait("vector", s_mm, xp_marks[q - 1])
                for i in range(4):
                    g = 4 * q + i
                    ut = ar[:, M_UQ + 2048 * i: M_UQ + 2048 * (i + 1)]
                    xcp = ar[:, M_XCP + 2051 * i: M_XCP + 2051 * (i + 1)]
                    p.run("vector",
                          lambda e, u=ut, x=xcp, g=g: e.tensor_scalar(
                              u, x[:, 0:2048], wcv[:, 4 * g:4 * g + 1],
                              None, ALU.mult), inc=s_ve)
                    for j in range(1, 4):
                        p.wait("vector", s_ve, s_ve.total)
                        p.run("vector",
                              lambda e, u=ut, x=xcp, g=g, j=j:
                              e.scalar_tensor_tensor(
                                  u, x[:, j:j + 2048],
                                  wcv[:, 4 * g + j:4 * g + j + 1], u,
                                  ALU.mult, ALU.add), inc=s_ve)
                    ve_u = s_ve.total
                    p.wait("scalar", s_ve, ve_u)
                    sact(ut, ut, ACT.Silu, bias=bcv[:, g:g + 1])
                sc_u_done = s_sc.total
                p.wait("gpsimd", s_sc, sc_u_done)
                dma("gpsimd", s_uo,
                    sc_u.rearrange("(a p) t -> p a t", p=128)
                    [:, 4 * q:4 * (q + 1), :],
                    ar[:, M_UQ:M_UQ + 8192].rearrange(
                        "p (a t) -> p a t", a=4))
                def emit_xproj(q=q, sc_u_done=sc_u_done, w_wx=w_wx):
                    xp0 = s_mm.total
                    _emit_xproj_body(q, sc_u_done, w_wx)
                    xp_marks[q] = s_mm.total
                pend_xproj.append(emit_xproj)
                m1_mark = s_mm.total
            p.wait("sync", s_mm, m1_mark)
            dma("sync", s_wdtl, ar[0:64, M_WDT:M_WDT + 2048], dwdt[:, :])
            w_wdt = s_wdtl.total
            s_xrl = p.sem("s_xrl")
            p.wait("gpsimd", s_mm, m1_mark)
            g_x2 = dma("gpsimd", s_xrl,
                       xres[:].rearrange("p (k t) -> p k t", k=8),
                       sc_x.rearrange("(k p) t -> p k t", p=128))
            while pend_xproj:
                pend_xproj.pop(0)()
            p.wait("tensor", s_xrl, g_x2)
            p.wait("vector", s_xrl, g_x2)
            p.barrier()

            s_url = [p.sem("s_url0"), p.sem("s_url1"), p.sem("s_url2"),
                     p.sem("s_url3")]
            url_gate = [0, 0]
            s_wo = p.sem("s_wo")
            wo_mark = 0
            m5_marks = []
            p.wait("tensor", s_wdtl, w_wdt)
            def emit_m3(q_, c_, du_gate):
                p.wait("tensor", s_sc, s_sc.total)
                for mt in range(4):
                    for nn in range(2):
                        bank = 2 * mt + nn
                        mm(pp[:, bank, :],
                           ar[0:64, M_WDT + 512 * q_ + 128 * mt:
                              M_WDT + 512 * q_ + 128 * (mt + 1)],
                           ar[0:64, M_XD + 1024 * c_ + 512 * nn:
                              M_XD + 1024 * c_ + 512 * (nn + 1)],
                           True, True,
                           inc=s_mm if (mt == 3 and nn == 1) else None)
                m_dt = s_mm.total
                p.wait("scalar", s_mm, m_dt)
                p.wait("scalar", s_ve, du_gate)
                dl = {}
                for mt in range(4):
                    g = 4 * q_ + mt
                    for nn in range(2):
                        bank = 2 * mt + nn
                        dsl = ar[:, M_DL + 1024 * mt + 512 * nn:
                                 M_DL + 1024 * mt + 512 * (nn + 1)]
                        sact(dsl, pp[:, bank, :], ACT.Exp,
                             bias=bdt[:, g:g + 1])
                        p.wait("scalar", s_sc, s_sc.total)
                        sact(dsl, dsl, ACT.Ln, bias=1.0)
                    dl[mt] = s_sc.total
                return dl, s_sc.total

            pend_dl = None
            for q in range(4):
                p.wait("sync", s_mm, wo_mark)
                for kq in range(4):
                    dma("sync", s_wo,
                        ar[:, M_WSL + 1024 * kq: M_WSL + 1024 * (kq + 1)],
                        dwout[512 * q + 128 * kq:512 * q + 128 * (kq + 1), :])
                w_wo = s_wo.total
                for c in range(2):
                    if pend_dl is None:
                        pend_dl = emit_m3(q, c, 0)
                    dl_marks, sc_dl = pend_dl
                    pend_dl = None
                    ve_start = s_ve.total
                    du_done = [ve_start]
                    bc_done = {}
                    bc_evac_prev = [sc_dl]
                    bc_ready = {}

                    def build_bc(n):
                        par = n % 2
                        bcb_o = M_BCB if par == 0 else 22784
                        bcc_o = M_BCC if par == 0 else 23808
                        fb = ve_start if par == 0 else du_done[0]
                        p.wait("tensor", s_ve, bc_done.get(n - 2, fb))
                        p.wait("tensor", s_sc, bc_evac_prev[0])
                        for half in range(2):
                            if half == 1:
                                p.wait("tensor", s_sc, bc_evac_prev[0])
                            mm(pp[:, 6, :],
                               sel[:, 128 * n:128 * (n + 1)],
                               ar[0:16, M_RB + 1024 * c + 512 * half:
                                  M_RB + 1024 * c + 512 * (half + 1)],
                               True, True, inc=s_mm)
                            mm(pp[:, 7, :],
                               sel[:, 128 * n:128 * (n + 1)],
                               ar[0:16, M_RC + 1024 * c + 512 * half:
                                  M_RC + 1024 * c + 512 * (half + 1)],
                               True, True, inc=s_mm)
                            m_bc2 = s_mm.total
                            p.wait("scalar", s_mm, m_bc2)
                            sact(ar[:, bcb_o + 512 * half:
                                    bcb_o + 512 * (half + 1)],
                                 pp[:, 6, :], ACT.Copy)
                            sact(ar[:, bcc_o + 512 * half:
                                    bcc_o + 512 * (half + 1)],
                                 pp[:, 7, :], ACT.Copy)
                            bc_evac_prev[0] = s_sc.total
                        bc_ready[n] = s_sc.total

                    build_bc(0)
                    p.wait("sync", s_ve, s_ve.total)
                    p.wait("sync", s_sc, s_sc.total)
                    STG4 = [M_STG, M_STG + 1024, M_DA[0], M_DA[1]]
                    uvs = []
                    for i in range(4):
                        g = 4 * q + i
                        ustg = ar[:, STG4[i]:STG4[i] + 1024]
                        v = dma("sync", s_url[i], ustg,
                                sc_u[128 * g:128 * (g + 1),
                                     1024 * c:1024 * (c + 1)])
                        uvs.append((ustg, v, i))
                    for i in range(4):
                        g = 4 * q + i
                        ustg, v, j = uvs[i]
                        p.wait("vector", s_url[j], v)
                        p.wait("vector", s_sc, dl_marks[i])
                        p.run("vector",
                              lambda e, i=i, u=ustg: e.tensor_mul(
                                  ar[:, M_DU + 1024 * i: M_DU + 1024 * (i + 1)],
                                  ar[:, M_DL + 1024 * i: M_DL + 1024 * (i + 1)],
                                  u), inc=s_ve)
                        p.run("vector",
                              lambda e, i=i, u=ustg, g=g: e.tensor_scalar(
                                  ar[:, M_Y + 1024 * i: M_Y + 1024 * (i + 1)],
                                  u, dvc[:, g:g + 1], None, ALU.mult),
                              inc=s_ve)
                    url_gate[0] = s_ve.total
                    url_gate[1] = s_ve.total
                    du_done[0] = s_ve.total
                    da_gate[0] = max(da_gate[0], s_ve.total)
                    da_gate[1] = max(da_gate[1], s_ve.total)
                    for n in range(DS):
                        par = n % 2
                        bcb_o = M_BCB if par == 0 else 22784
                        bcc_o = M_BCC if par == 0 else 23808
                        for i in range(4):
                            g = 4 * q + i
                            da = ar[:, M_DA[i % 2]:M_DA[i % 2] + 1024]
                            p.wait("scalar", s_ve, da_gate[i % 2])
                            p.run("scalar",
                                  lambda e, da=da, i=i, g=g, n=n:
                                  e.activation(
                                      da,
                                      ar[:, M_DL + 1024 * i:
                                         M_DL + 1024 * (i + 1)],
                                      ACT.Exp,
                                      scale=acol[:, 16 * g + n:
                                                 16 * g + n + 1]),
                                  inc=s_sc)
                            sc_da = s_sc.total
                            p.wait("vector", s_sc, bc_ready[n])
                            p.wait("vector", s_ve, s_ve.total)
                            p.run("vector",
                                  lambda e, i=i, bo=bcb_o: e.tensor_mul(
                                      ar[:, M_W:M_W + 1024],
                                      ar[:, M_DU + 1024 * i:
                                         M_DU + 1024 * (i + 1)],
                                      ar[:, bo:bo + 1024]), inc=s_ve)
                            p.wait("vector", s_sc, sc_da)
                            p.wait("vector", s_ve, s_ve.total)
                            init = (0.0 if c == 0 else
                                    state[:, 16 * i + n:16 * i + n + 1])
                            p.run("vector",
                                  lambda e, da=da, init=init:
                                  e.tensor_tensor_scan(
                                      ar[:, M_H:M_H + 1024], da,
                                      ar[:, M_W:M_W + 1024], init,
                                      ALU.mult, ALU.add), inc=s_ve)
                            da_gate[i % 2] = s_ve.total
                            if c == 0:
                                p.wait("vector", s_ve, s_ve.total)
                                p.run("vector",
                                      lambda e, i=i, n=n: e.tensor_copy(
                                          state[:, 16 * i + n:
                                                16 * i + n + 1],
                                          ar[:, M_H + 1023:M_H + 1024]),
                                      inc=s_ve)
                            p.wait("vector", s_ve, s_ve.total)
                            p.run("vector",
                                  lambda e, co=bcc_o: e.tensor_mul(
                                      ar[:, M_W:M_W + 1024],
                                      ar[:, M_H:M_H + 1024],
                                      ar[:, co:co + 1024]), inc=s_ve)
                            p.wait("vector", s_ve, s_ve.total)
                            p.run("vector",
                                  lambda e, i=i: e.tensor_add(
                                      ar[:, M_Y + 1024 * i:
                                         M_Y + 1024 * (i + 1)],
                                      ar[:, M_Y + 1024 * i:
                                         M_Y + 1024 * (i + 1)],
                                      ar[:, M_W:M_W + 1024]), inc=s_ve)
                            if i == 1 and n + 1 < DS:
                                build_bc(n + 1)
                        bc_done[n] = s_ve.total
                    if not (q == 3 and c == 1):
                        nq_, ncc_ = (q, 1) if c == 0 else (q + 1, 0)
                        pend_dl = emit_m3(nq_, ncc_, du_done[0])
                        dl_next_sc = pend_dl[1]
                    else:
                        dl_next_sc = 0
                    p.wait("sync", s_ve, s_ve.total)
                    p.wait("sync", s_sc, s_sc.total)
                    STG4 = [M_STG, M_STG + 1024, M_DA[0], M_DA[1]]
                    rvs = []
                    for i in range(4):
                        g = 4 * q + i
                        ustg = ar[:, STG4[i]:STG4[i] + 1024]
                        v = dma("sync", s_url[i], ustg,
                                sc_res[128 * g:128 * (g + 1),
                                       1024 * c:1024 * (c + 1)])
                        rvs.append((ustg, v))
                    for i in range(4):
                        ustg, v = rvs[i]
                        p.wait("vector", s_url[i], v)
                        p.wait("vector", s_ve, s_ve.total)
                        p.run("vector",
                              lambda e, i=i, u=ustg: e.tensor_mul(
                                  ar[:, M_Y + 1024 * i: M_Y + 1024 * (i + 1)],
                                  ar[:, M_Y + 1024 * i: M_Y + 1024 * (i + 1)],
                                  u), inc=s_ve)
                    url_gate[0] = s_ve.total
                    url_gate[1] = s_ve.total
                    da_gate[0] = max(da_gate[0], s_ve.total)
                    da_gate[1] = max(da_gate[1], s_ve.total)
                    ve_yo = s_ve.total
                    p.wait("tensor", s_ve, ve_yo)
                    p.wait("tensor", s_wo, w_wo)
                    p.wait("tensor", s_sc, dl_next_sc)
                    for nn in range(2):
                        for kq in range(4):
                            for m in range(8):
                                mm(pp[:, m, :],
                                   ar[:, M_WSL + 1024 * kq + 128 * m:
                                      M_WSL + 1024 * kq + 128 * (m + 1)],
                                   ar[:, M_Y + 1024 * kq + 512 * nn:
                                      M_Y + 1024 * kq + 512 * (nn + 1)],
                                   kq == 0, kq == 3,
                                   inc=s_mm if (kq == 3 and m == 7)
                                   else None)
                        m_wo = s_mm.total
                        p.wait("vector", s_mm, m_wo)
                        for m in range(8):
                            p.run("vector",
                                  lambda e, m=m, c=c, nn=nn:
                                  e.tensor_add(
                                      xres[:, m * T + 1024 * c + 512 * nn:
                                           m * T + 1024 * c + 512 * (nn + 1)],
                                      xres[:, m * T + 1024 * c + 512 * nn:
                                           m * T + 1024 * c + 512 * (nn + 1)],
                                      pp[:, m, :]), inc=s_ve)
                            if q == 3 and c == 1 and nn == 1:
                                m5_marks.append(s_ve.total)
                        p.wait("tensor", s_ve, s_ve.total)
                wo_mark = s_mm.total

        if stop == "x4":
            emit_ffn("f2", src_marks=m5_marks)

        p.check()

        p.wait("gpsimd", s_ve, s_ve.total)
        p.wait("gpsimd", s_sc, s_sc.total)
        dma("gpsimd", s_st, dout.rearrange("(k p) t -> p k t", p=128),
            xres[:].rearrange("p (k t) -> p k t", k=8))
        p.wait("gpsimd", s_st, s_st.total)
        p.check()

        for s in p.sems.values():
            s.h = stk.enter_context(nc.semaphore(s.name))
        block = stk.enter_context(nc.Block())

        for eng in Prog.ENGINES:
            def make(eng):
                def body(e):
                    for f in p.ops[eng]:
                        f(e)
                return body
            getattr(block, eng)(make(eng))
    finally:
        stk.close()
    return nc


def _sel_np():
    s = np.zeros((16, 2048), dtype=np.float32)
    for j in range(16):
        s[j, 128 * j:128 * (j + 1)] = 1.0
    return s


def kernel(**inputs):
    d = inputs
    x = np.ascontiguousarray(d["L0_attn_a_x"]).astype(np.float32, copy=False)
    base = {
        "wqkvT": np.ascontiguousarray(d["L0_attn_a_qkv_w"].T),
        "wprojT": np.ascontiguousarray(d["L0_attn_a_proj_w"].T),
        "qkvb": np.ascontiguousarray(d["L0_attn_a_qkv_b"]),
        "projb": np.ascontiguousarray(d["L0_attn_a_proj_b"]),
        "nw": np.ascontiguousarray(d["L0_attn_a_norm_w"]),
        "nb": np.ascontiguousarray(d["L0_attn_a_norm_b"]),
        "wgT": np.ascontiguousarray(d["L0_ffn_w_gate"].T),
        "wuT": np.ascontiguousarray(d["L0_ffn_w_up"].T),
        "wdT": np.ascontiguousarray(d["L0_ffn_w_down"].T),
        "winT": np.ascontiguousarray(d["L0_attn_b_W_in"].T),
        "woutT": np.ascontiguousarray(d["L0_attn_b_W_out"].T),
        "wxT": np.ascontiguousarray(d["L0_attn_b_W_xproj"].T),
        "wdtT": np.ascontiguousarray(d["L0_attn_b_W_dt"].T),
        "alog": np.ascontiguousarray(d["L0_attn_b_A_log"]),
        "wconv": np.ascontiguousarray(d["L0_attn_b_W_conv"][:, 0, :]),
        "bconv": np.ascontiguousarray(d["L0_attn_b_b_conv"]),
        "bdt": np.ascontiguousarray(d["L0_attn_b_b_dt"]),
        "dvec": np.ascontiguousarray(d["L0_attn_b_D"]),
        "cvals": np.tile(np.asarray(CVALS, dtype=np.float32), (128, 1)),
        "onesin": np.ones((128, 128), dtype=np.float32),
        "selin": _sel_np(),
    }
    nc = build_nc()
    in_maps = []
    for i in range(8):
        m = dict(base)
        m["xT"] = np.ascontiguousarray(x[i].T)
        in_maps.append(m)
    res = run_bass_kernel_spmd(nc, in_maps, core_ids=list(range(8)))
    out = np.stack([np.asarray(r["outT"]).T for r in res.results])
    return out.astype(np.float32)


if __name__ == "__main__":
    pass
